# version 16
# baseline (speedup 1.0000x reference)
import numpy as np
from contextlib import ExitStack

import concourse.bass as bass
import concourse.mybir as mybir
from concourse.bass_utils import run_bass_kernel_spmd

F32 = mybir.dt.float32
BF16 = mybir.dt.bfloat16
OP = mybir.AluOpType
AF = mybir.ActivationFunctionType
AX = mybir.AxisListType

D = 1024
HH = 16
HD = 64
CH = 64
FF = 4096
NE = 8
NR = 16
NCORES = 8


class Sem:
    def __init__(self, name, dma=False):
        self.name = name
        self.n = 0
        self.h = None
        self.dma = dma
        self.last_unmarked = None

    def mark(self):
        if self.last_unmarked is not None:
            self.last_unmarked[2][0] = True
            self.n += self.last_unmarked[1]
            self.last_unmarked = None


def build(T=4096, debug=False, upto=9, p3n=-1, p3cut="", sparse=True, p4n=-1):
    nc = bass.Bass("TRN2", target_bir_lowering=False)
    NT = T // 128
    NG = T // 512
    NCH = T // CH

    def inp(name, shape):
        return nc.dram_tensor(name, list(shape), F32, kind="ExternalInput").ap()

    x_in = inp("x", (T, D))
    wq_in = inp("wq", (D, D))
    wk_in = inp("wk", (D, D))
    wv_in = inp("wv", (D, D))
    wo_in = inp("wo", (D, D))
    w1_in = inp("w1s", (HD, HH * HD))
    b1_in = inp("b1s", (1, HH * HD))
    th_in = inp("thetas", (HD, HH))
    tb_in = inp("tbs", (1, HH))
    al_in = inp("alphas", (CH, 1))
    ga_in = inp("gammas", (1, HH * HD))
    be_in = inp("betas", (1, HH * HD))
    gw_in = inp("gwt", (D, NE))
    wg_in = inp("wgt", (D, FF))
    wu_in = inp("wut", (D, FF))
    wd_in = inp("wdt", (FF, D))
    ac_in = inp("acomb", (D, NE * 64))
    b1p_in = inp("b1p", (NR, NE * FF))
    b3p_in = inp("b3p", (NR, NE * FF))
    a2_in = inp("a2sc", (FF, NE * NR))
    b2_in = inp("b2tp", (2 * 128, D))
    id_in = inp("identc", (128, 128))
    ones_in = inp("onesc", (128, 128))
    inv_in = inp("invkc", (128, 1))
    pat8_in = inp("pat8c", (NE, NE * 128))
    pat16_in = inp("pat16c", (80, HH * HD))

    y_out = nc.dram_tensor("y", [T, D], F32, kind="ExternalOutput").ap()
    dbg = {}
    if debug:
        dbg["xn"] = nc.dram_tensor("dbg_xn", [T, D], F32, kind="ExternalOutput").ap()
        dbg["w"] = nc.dram_tensor("dbg_w", [T, NE], F32, kind="ExternalOutput").ap()
        dbg["st"] = nc.dram_tensor("dbg_st", [D, T], F32, kind="ExternalOutput").ap()

    def scr(name, shape, dt=BF16):
        return nc.dram_tensor(name, list(shape), dt, kind="Internal").ap()

    wg_sc = scr("wg_sc", (D, FF))
    wu_sc = scr("wu_sc", (D, FF))
    wd_sc = scr("wd_sc", (FF, D))
    wo_sc = scr("wo_sc", (D, D))
    wq_sc = scr("wq_sc", (D, D))
    wk_sc = scr("wk_sc", (D, D))
    wv_sc = scr("wv_sc", (D, D))
    a2_sc = scr("a2_sc", (FF, NE * NR))
    b1pos_sc = scr("b1pos_sc", (NR, NE * FF))
    b3pos_sc = scr("b3pos_sc", (NR, NE * FF))
    st_sc = scr("st_sc", (D, T), F32)
    xnT_sc = scr("xnT_sc", (128, 8 * T))

    SP = Sem("sp"); SV = Sem("sv"); SA = Sem("sa"); SG = Sem("sg")
    DS = Sem("ds", dma=True); DA = Sem("da", dma=True); DG = Sem("dg", dma=True)
    DW = Sem("dw", dma=True); DW1 = Sem("dw1", dma=True); DW2 = Sem("dw2", dma=True)
    DD0 = Sem("dd0", dma=True); DD1 = Sem("dd1", dma=True)
    DGa = Sem("dga", dma=True); DGb = Sem("dgb", dma=True); DAC = Sem("dac", dma=True)
    SEMS = [SP, SV, SA, SG, DS, DA, DG, DW, DW1, DW2, DD0, DD1, DGa, DGb, DAC]
    EK = ("t", "v", "a", "g", "s")

    cur = {"plan": None, "lastw": None}
    prd = {}

    def wt(key, sem, val=None):
        if val is None:
            sem.mark()
            v = sem.n
        else:
            v = val
        if v <= 0:
            return
        lw = cur["lastw"][key]
        if lw.get(sem.name, -1) >= v:
            return
        lw[sem.name] = v
        cur["plan"][key].append(lambda e, s=sem, v=v: e.wait_ge(s.h, v))

    def em(key, fn, sem=None, k=1, wb=(), rp=()):
        if key == "t":
            for t_ in wb:
                info = prd.get(t_.name)
                if info:
                    if info["v"]:
                        wt("t", SV, info["v"])
                    if info["a"]:
                        wt("t", SA, info["a"])
        elif key in ("v", "a") and rp:
            other = "a" if key == "v" else "v"
            osem = SA if key == "v" else SV
            for t_ in rp:
                info = prd.get(t_.name)
                if info and info[other]:
                    wt(key, osem, info[other])
        if sem is None:
            cur["plan"][key].append(fn)
        elif sem.dma or not sparse:
            if not sem.dma:
                sem.mark()
                sem.n += k
                cur["plan"][key].append(lambda e, f=fn, s=sem, k=k: f(e).then_inc(s.h, k))
            else:
                sem.n += k
                cur["plan"][key].append(lambda e, f=fn, s=sem, k=k: f(e).then_inc(s.h, k))
        else:
            ent = [fn, k, [False]]
            sem.last_unmarked = ent
            cur["plan"][key].append(
                lambda e, f=fn, s=sem, k=k, fl=ent[2]: (f(e).then_inc(s.h, k) if fl[0] else f(e)))
        if key in ("v", "a") and rp and sem is not None:
            sem.mark()
            for t_ in rp:
                info = prd.setdefault(t_.name, {"v": 0, "a": 0})
                info[key] = sem.n

    def wtall(key):
        for s_ in SEMS:
            wt(key, s_)

    def barrier():
        for k in EK:
            wtall(k)

    def mk(sem):
        sem.mark()
        return sem.n

    class phase:
        def __init__(self, name):
            self.name = name

        def __enter__(self):
            self.es = ExitStack()
            self.es.__enter__()
            cur["plan"] = {k: [] for k in EK}
            cur["lastw"] = {k: {} for k in EK}
            return self.es

        def __exit__(self, *a):
            for s_ in SEMS:
                if not s_.dma:
                    s_.mark()
            plan = cur["plan"]
            with nc.Block() as block:
                block.tensor(lambda e: [f(e) for f in plan["t"]])
                block.vector(lambda e: [f(e) for f in plan["v"]])
                block.scalar(lambda e: [f(e) for f in plan["a"]])
                block.gpsimd(lambda e: [f(e) for f in plan["g"]])
                block.sync(lambda e: [f(e) for f in plan["s"]])
            cur["plan"] = None
            return self.es.__exit__(*a)

    class _Stop(Exception):
        pass

    es0 = ExitStack()
    try:
      with es0:
        def sb0(name, shape, dt=BF16):
            return es0.enter_context(nc.sbuf_tensor(name, list(shape), dt))

        for s_ in SEMS:
            s_.h = es0.enter_context(nc.semaphore(s_.name))
        wTb = sb0("wTb", (NE, T))
        identb = sb0("identb", (128, 128))
        ones1b = sb0("ones1b", (128, 128))
        acb = sb0("acb", (128, 8 * 512))
        pat8b = sb0("pat8b", (NE, NE * 128))
        pat16b = sb0("pat16b", (80, HH * HD))
        b2s = [sb0("b2s0", (128, D)), sb0("b2s1", (128, D))]

        mid = ExitStack()
        with mid:
            wqb = mid.enter_context(nc.sbuf_tensor("wqb", [128, 8 * D], BF16))
            wkb = mid.enter_context(nc.sbuf_tensor("wkb", [128, 8 * D], BF16))
            wvb = mid.enter_context(nc.sbuf_tensor("wvb", [128, 8 * D], BF16))
            attn_w_loaded = [False]

            with phase("p0") as pes:
                stg32 = pes.enter_context(nc.sbuf_tensor("stg32", [128, 2048], F32))
                stgbf = pes.enter_context(nc.sbuf_tensor("stgbf", [128, 2048], BF16))

                def conv(src_ap, rows, cols, dst_sb=None, dst_dr=None, scale=None,
                         dst_sb_colw=None):
                    for r0 in range(0, rows, 128):
                        rr = min(128, rows - r0)
                        for c0 in range(0, cols, 2048):
                            cc = min(2048, cols - c0)
                            wt("a", SG)
                            wt("a", SV)
                            em("a", lambda e, r0=r0, c0=c0, rr=rr, cc=cc: e.dma_start(
                                out=stg32[0:rr, 0:cc], in_=src_ap[r0:r0 + rr, c0:c0 + cc]), DA, 16)
                            wt("g", DA)
                            if scale is None:
                                em("g", lambda e, rr=rr, cc=cc: e.tensor_copy(
                                    out=stgbf[0:rr, 0:cc], in_=stg32[0:rr, 0:cc]), SG)
                            else:
                                em("g", lambda e, rr=rr, cc=cc, sc=scale: e.tensor_scalar_mul(
                                    out=stgbf[0:rr, 0:cc], in0=stg32[0:rr, 0:cc], scalar1=sc), SG)
                            if dst_dr is not None:
                                wt("a", SG)
                                em("a", lambda e, r0=r0, c0=c0, rr=rr, cc=cc: e.dma_start(
                                    out=dst_dr[r0:r0 + rr, c0:c0 + cc], in_=stgbf[0:rr, 0:cc]), DA, 16)
                            else:
                                wt("v", SG)
                                cw = cols if dst_sb_colw is None else dst_sb_colw
                                em("v", lambda e, r0=r0, c0=c0, rr=rr, cc=cc, cw=cw: e.tensor_copy(
                                    out=dst_sb[0:rr, (r0 // 128) * cw + c0:(r0 // 128) * cw + c0 + cc],
                                    in_=stgbf[0:rr, 0:cc]), SV)

                em("s", lambda e: e.dma_start(out=stg32[:, 0:128], in_=id_in), DS, 16)
                em("s", lambda e: e.dma_start(out=stg32[:, 128:256], in_=ones_in), DS, 16)
                em("s", lambda e: e.dma_start(out=stg32[0:NE, 256:256 + NE * 128], in_=pat8_in), DS, 16)
                wt("v", DS)
                em("v", lambda e: e.tensor_copy(out=identb[:], in_=stg32[:, 0:128]), SV)
                em("v", lambda e: e.tensor_copy(out=ones1b[:], in_=stg32[:, 128:256]), SV)
                em("v", lambda e: e.tensor_copy(out=pat8b[:], in_=stg32[0:NE, 256:256 + NE * 128]), SV)
                wt("s", SV)
                em("s", lambda e: e.dma_start(out=stg32[0:80, 0:1024], in_=pat16_in), DS, 16)
                wt("v", DS)
                em("v", lambda e: e.tensor_copy(out=pat16b[:], in_=stg32[0:80, 0:1024]), SV)

                for s_ap, d_ap in ((wq_in, wq_sc), (wk_in, wk_sc), (wv_in, wv_sc),
                                   (wo_in, wo_sc), (wg_in, wg_sc), (wu_in, wu_sc),
                                   (wd_in, wd_sc), (a2_in, a2_sc),
                                   (b1p_in, b1pos_sc), (b3p_in, b3pos_sc)):
                    em("g", lambda e, s_ap=s_ap, d_ap=d_ap: e.dma_start(out=d_ap, in_=s_ap), DG, 16)
                for r0 in range(0, D, 128):
                    wt("a", SV)
                    em("a", lambda e, r0=r0: e.dma_start(
                        out=stg32[:, 0:512], in_=ac_in[r0:r0 + 128, :]), DA, 16)
                    wt("v", DA)
                    vv = lambda t: t.rearrange("p (e a b r) -> p e a b r", e=8, a=2, b=2)
                    em("v", lambda e, vv=vv: e.tensor_scalar_mul(
                        out=vv(stgbf[:, 0:512])[:, :, :, 0:1, :],
                        in0=vv(stg32[:, 0:512])[:, :, :, 0:1, :], scalar1=2.0), SV)
                    wt("v", SV)
                    em("v", lambda e, vv=vv: e.tensor_scalar_mul(
                        out=vv(stgbf[:, 0:512])[:, :, :, 1:2, :],
                        in0=vv(stg32[:, 0:512])[:, :, :, 1:2, :], scalar1=-2.0), SV)
                    wt("v", SV)
                    em("v", lambda e, r0=r0: e.tensor_copy(
                        out=acb[:, (r0 // 128) * 512:(r0 // 128) * 512 + 512],
                        in_=stgbf[:, 0:512]), SV)
                conv(b2_in[0:128, :], 128, D, dst_sb=b2s[0], scale=2.0)
                conv(b2_in[128:256, :], 128, D, dst_sb=b2s[1], scale=2.0)
                barrier()

            if upto < 1:
                raise _Stop
            with phase("p1") as pes:
                lnx = [pes.enter_context(nc.sbuf_tensor(f"lnx{i}", [128, D], F32)) for i in range(2)]
                lnsq = [pes.enter_context(nc.sbuf_tensor(f"lnsq{i}", [128, D], F32)) for i in range(2)]
                lns = [pes.enter_context(nc.sbuf_tensor(f"lns{i}", [128, 16], F32)) for i in range(2)]
                lnxn = pes.enter_context(nc.sbuf_tensor("lnxn", [128, D], BF16))
                xout = [pes.enter_context(nc.sbuf_tensor(f"xout{i}", [128, D], BF16)) for i in range(2)]
                xn32 = [pes.enter_context(nc.sbuf_tensor(f"xn32{i}", [128, D], F32)) for i in range(2)]
                xnt32 = [pes.enter_context(nc.sbuf_tensor(f"xnt32{i}", [128, D], F32)) for i in range(2)]
                identf1 = pes.enter_context(nc.sbuf_tensor("identf1", [128, 128], F32))
                gwb = pes.enter_context(nc.sbuf_tensor("gwb", [128, 8 * NE], F32))
                wtok = [pes.enter_context(nc.sbuf_tensor(f"wtok{i}", [128, 48], F32)) for i in range(2)]
                psT = pes.enter_context(nc.psum_tensor("psT", [128, 1024], BF16))
                psTF = [pes.enter_context(nc.psum_tensor(f"psTF{i}", [128, 1024], F32)) for i in range(2)]
                psD = [pes.enter_context(nc.psum_tensor(f"psD{i}", [128, 512], F32)) for i in range(2)]
                epsc = pes.enter_context(nc.sbuf_tensor("epsc", [128, 1], F32))
                em("v", lambda e: e.memset(epsc[:], 1e-5), SV)
                em("s", lambda e: e.dma_start(out=identf1[:], in_=id_in), DS, 16)

                em("s", lambda e: e.dma_start(
                    out=gwb[:].rearrange("p (a b) -> p a b", a=8),
                    in_=gw_in.rearrange("(a p) b -> p a b", p=128)), DS, 16)

                mks = {}

                def setm(name, sem):
                    mks[name] = (sem, mk(sem))

                def wtm(key, name):
                    if name in mks:
                        s_, v_ = mks[name]
                        if s_.dma:
                            wt(key, s_)
                        else:
                            wt(key, s_, v_)

                for it in range(NT):
                    pp = it % 2
                    wtm("s", f"lnxfree{pp}")
                    em("s", lambda e, it=it, pp=pp: e.dma_start(
                        out=lnx[pp][:], in_=x_in[it * 128:(it + 1) * 128, :]), DS, 16)
                    dsv = DS.n
                    wt("a", DS, dsv)
                    wtm("a", f"sqfree{pp}")
                    em("a", lambda e, pp=pp: e.activation(out=lnsq[pp][:], in_=lnx[pp][:], func=AF.Square), SA)
                    setm(f"sq{pp}", SA)
                    wt("v", DS, dsv)
                    wtm("v", f"lnsfree{pp}")
                    em("v", lambda e, pp=pp: e.reduce_sum(
                        out=lns[pp][:, 0:1].rearrange("p (s o) -> p s o", o=1),
                        in_=lnx[pp][:].rearrange("p (a b) -> p a b", a=1), axis=AX.X), SV)
                    wtm("v", f"sq{pp}")
                    em("v", lambda e, pp=pp: e.reduce_sum(
                        out=lns[pp][:, 1:2].rearrange("p (s o) -> p s o", o=1),
                        in_=lnsq[pp][:].rearrange("p (a b) -> p a b", a=1), axis=AX.X), SV)
                    setm(f"sqfree{pp}", SV)
                    wt("v", SV)
                    em("v", lambda e, pp=pp: e.tensor_scalar_mul(out=lns[pp][:, 2:3], in0=lns[pp][:, 0:1], scalar1=1.0 / D), SV)
                    em("v", lambda e, pp=pp: e.tensor_scalar_mul(out=lns[pp][:, 3:4], in0=lns[pp][:, 1:2], scalar1=1.0 / D), SV)
                    wt("v", SV)
                    em("v", lambda e, pp=pp: e.tensor_mul(out=lns[pp][:, 4:5], in0=lns[pp][:, 2:3], in1=lns[pp][:, 2:3]), SV)
                    wt("v", SV)
                    em("v", lambda e, pp=pp: e.tensor_sub(out=lns[pp][:, 5:6], in0=lns[pp][:, 3:4], in1=lns[pp][:, 4:5]), SV)
                    svv = mk(SV)
                    wt("a", SV, svv)
                    em("a", lambda e, pp=pp: e.activation(out=lns[pp][:, 6:7], in_=lns[pp][:, 5:6], func=AF.Sqrt, bias=epsc[:]), SA)
                    wt("v", SA)
                    em("v", lambda e, pp=pp: e.reciprocal(out=lns[pp][:, 7:8], in_=lns[pp][:, 6:7]), SV)
                    wt("v", SV)
                    wtm("v", f"x32free{pp}")
                    em("v", lambda e, pp=pp: e.tensor_scalar(
                        out=xn32[pp][:], in0=lnx[pp][:], scalar1=lns[pp][:, 2:3], scalar2=lns[pp][:, 7:8],
                        op0=OP.subtract, op1=OP.mult), SV)
                    setm(f"lnxfree{pp}", SV)
                    setm(f"lnsfree{pp}", SV)
                    if debug:
                        wt("s", SV)
                        em("s", lambda e, it=it, pp=pp: e.dma_start(
                            out=dbg["xn"][it * 128:(it + 1) * 128, :], in_=xn32[pp][:]), DS, 16)
                    x32v = mk(SV)
                    wt("t", SV, x32v)
                    for db in range(8):
                        em("t", lambda e, db=db, pp=pp: e.transpose(
                            psTF[pp][:, 128 * db:128 * (db + 1)],
                            xn32[pp][:, 128 * db:128 * (db + 1)],
                            identf1[:]), SP, wb=(psTF[pp],))
                    setm(f"x32free{pp}", SP)
                    tfv = mk(SP)
                    wt("v", SP, tfv)
                    wtm("v", f"xoutfree{pp}")
                    wtm("v", f"xnt32free{pp}")
                    em("v", lambda e, pp=pp: e.tensor_copy(out=xout[pp][:], in_=psTF[pp][:]), SV, rp=(psTF[pp],))
                    em("v", lambda e, pp=pp: e.tensor_copy(out=xnt32[pp][:], in_=psTF[pp][:]), SV, rp=(psTF[pp],))
                    xot = mk(SV)
                    wt("a", SV, xot)
                    em("a", lambda e, it=it, pp=pp: e.dma_start(
                        out=xnT_sc.rearrange("p (a b) -> p a b", a=8)[:, :, it * 128:(it + 1) * 128],
                        in_=xout[pp][:].rearrange("p (a b) -> p a b", a=8)), DA, 16)
                    setm(f"xoutfree{pp}", DA)

                    wt("t", SV, xot)
                    for db in range(8):
                        em("t", lambda e, db=db, pp=pp: e.matmul(
                            psD[pp][:, 0:8],
                            xnt32[pp][:, 128 * db:128 * (db + 1)],
                            gwb[:, 8 * db:8 * (db + 1)],
                            start=(db == 0), stop=(db == 7)), SP, wb=(psD[pp],))
                    setm(f"xnt32free{pp}", SP)
                    gv = mk(SP)
                    g = wtok[pp]
                    pd = psD[pp]
                    wt("v", SP, gv)
                    em("v", lambda e, g=g, pd=pd: e.reduce_max(
                        out=g[:, 0:1].rearrange("p (s o) -> p s o", o=1),
                        in_=pd[:, 0:8].rearrange("p (a b) -> p a b", a=1), axis=AX.X), SV, rp=(pd,))
                    wt("v", SV)
                    em("v", lambda e, g=g, pd=pd: e.tensor_scalar(
                        out=g[:, 8:16], in0=pd[:, 0:8], scalar1=g[:, 0:1], scalar2=None,
                        op0=OP.is_ge), SV, rp=(pd,))
                    wt("v", SV)
                    em("v", lambda e, g=g, pd=pd: e.scalar_tensor_tensor(
                        out=g[:, 16:24], in0=g[:, 8:16], scalar=-1e9, in1=pd[:, 0:8],
                        op0=OP.mult, op1=OP.add), SV, rp=(pd,))
                    wt("v", SV)
                    em("v", lambda e, g=g: e.reduce_max(
                        out=g[:, 1:2].rearrange("p (s o) -> p s o", o=1),
                        in_=g[:, 16:24].rearrange("p (a b) -> p a b", a=1), axis=AX.X), SV)
                    wt("v", SV)
                    em("v", lambda e, g=g: e.tensor_add(out=g[:, 2:3], in0=g[:, 0:1], in1=g[:, 1:2]), SV)
                    wt("v", SV)
                    em("v", lambda e, g=g, pd=pd: e.tensor_scalar(
                        out=g[:, 24:32], in0=pd[:, 0:8], scalar1=2.0, scalar2=g[:, 2:3],
                        op0=OP.mult, op1=OP.subtract), SV, rp=(pd,))
                    sgv = mk(SV)
                    wt("a", SV, sgv)
                    em("a", lambda e, g=g: e.activation(out=g[:, 32:40], in_=g[:, 24:32], func=AF.Sigmoid), SA)
                    wt("v", SV)
                    em("v", lambda e, g=g, pd=pd: e.tensor_scalar(
                        out=g[:, 40:48], in0=pd[:, 0:8], scalar1=g[:, 1:2], scalar2=None,
                        op0=OP.is_ge), SV, rp=(pd,))
                    wt("v", SA)
                    wt("v", SV)
                    wtm("v", f"lnxnfree{pp}")
                    em("v", lambda e, g=g, pp=pp: e.tensor_mul(out=lnxn[:, 8 * pp:8 * pp + 8],
                                                               in0=g[:, 32:40], in1=g[:, 40:48]), SV)
                    if debug:
                        wt("v", SV)
                        em("v", lambda e, g=g: e.tensor_mul(out=g[:, 0:8], in0=g[:, 32:40], in1=g[:, 40:48]), SV)
                        wt("s", SV)
                        em("s", lambda e, it=it, g=g: e.dma_start(
                            out=dbg["w"][it * 128:(it + 1) * 128, :], in_=g[:, 0:8]), DS, 16)
                    wv_ = mk(SV)
                    wt("t", SV, wv_)
                    em("t", lambda e, pp=pp: e.transpose(
                        psT[0:NE, 128 * pp:128 * (pp + 1)], lnxn[:, 8 * pp:8 * pp + 8],
                        identb[:]), SP, wb=(psT,))
                    setm(f"lnxnfree{pp}", SP)
                    trv = mk(SP)
                    wt("v", SP, trv)
                    em("v", lambda e, it=it, pp=pp: e.tensor_copy(
                        out=wTb[:, it * 128:(it + 1) * 128], in_=psT[0:NE, 128 * pp:128 * (pp + 1)]), SV, rp=(psT,))
                barrier()

            if upto < 3:
                raise _Stop
            with phase("p3") as pes:
                def sbp(name, shape, dt=BF16):
                    return pes.enter_context(nc.sbuf_tensor(name, list(shape), dt))

                xng = sbp("xng", (128, 8 * 512))
                qtok = sbp("qtok", (128, 4 * D))
                ktok = sbp("ktok", (128, 4 * D))
                vtok = sbp("vtok", (128, 4 * D))
                qT0 = sbp("qT0", (64, HH * 512))
                kT0 = sbp("kT0", (64, HH * 512))
                outT = sbp("outT", (128, 8 * 512))
                Wb32 = sbp("Wb32", (65, D), F32)
                Wba = sbp("Wba", (65, D))
                thetab = sbp("thetab", (64, HH))
                tbb = sbp("tbb", (1, HH))
                csml = sbp("csml", (128, 8), F32)
                gBc = sbp("gBc", (128, D))
                betaBc = sbp("betaBc", (128, D))
                sq_s = sbp("sq_s", (128, D), F32)
                stt = sbp("stt", (128, 128), F32)
                lrs = sbp("lrs", (128, 80), F32)
                smf = sbp("smf", (128, 80), F32)
                smb = sbp("smb", (128, 80))
                smT = sbp("smT", (80, 64))
                csb = sbp("csb", (128, 48))
                csT = sbp("csT", (48, 128))
                c0B = sbp("c0B", (128, D))
                c1B = sbp("c1B", (128, D))
                xh1 = sbp("xh1", (128, D))
                xhat = sbp("xhat", (128, D))
                tmp1 = sbp("tmp1", (128, D))
                tmp2 = sbp("tmp2", (128, D))
                e2g = sbp("e2g", (128, D))
                gp = sbp("gp", (128, 2 * D))
                g2Bc = sbp("g2Bc", (128, D))
                gxp = sbp("gxp", (128, D))
                wrm = sbp("wrm", (1, 16), F32)
                em("v", lambda e: e.memset(wrm[:], 1.0), SV)
                aB = sbp("aB", (128, D))
                bB = sbp("bB", (128, D))
                cB = sbp("cB", (128, D))
                wost = sbp("wost", (128, 8 * 128))
                woev = sbp("woev", (128, 512), F32)
                stgs = sbp("stgs", (128, 1088), F32)
                psA = pes.enter_context(nc.psum_tensor("psA", [128, 1024], F32))
                psB = pes.enter_context(nc.psum_tensor("psB", [128, 1024], F32))
                psC = pes.enter_context(nc.psum_tensor("psC", [128, 1024], F32))
                psD = pes.enter_context(nc.psum_tensor("psD3", [128, 512], F32))
                psT = pes.enter_context(nc.psum_tensor("psT3", [128, 1024], BF16))
                epsc3 = pes.enter_context(nc.sbuf_tensor("epsc3", [128, 1], F32))

                for s_sc, d_sb in ((wq_sc, wqb), (wk_sc, wkb), (wv_sc, wvb)):
                    em("s", lambda e, s_sc=s_sc, d_sb=d_sb: e.dma_start(
                        out=d_sb[:].rearrange("p (a b) -> p a b", a=8),
                        in_=s_sc.rearrange("(a p) b -> p a b", p=128)), DS, 16)
                em("v", lambda e: e.memset(epsc3[:], 1e-6), SV)
                em("s", lambda e: e.dma_start(out=Wb32[0:64, :], in_=w1_in), DS, 16)
                em("s", lambda e: e.dma_start(out=Wb32[64:65, :], in_=b1_in), DS, 16)
                em("s", lambda e: e.dma_start(out=stgs[0:64, 0:16], in_=th_in), DS, 16)
                em("s", lambda e: e.dma_start(out=stgs[0:1, 16:32], in_=tb_in), DS, 16)
                em("s", lambda e: e.dma_start(out=csml[:, 0:1], in_=inv_in), DS, 16)
                em("s", lambda e: e.dma_start(out=csml[0:64, 1:2], in_=al_in), DS, 16)
                em("s", lambda e: e.dma_start(out=csml[64:128, 1:2], in_=al_in), DS, 16)
                em("s", lambda e: e.dma_start(out=stgs[0:1, 32:32 + D], in_=ga_in), DS, 16)
                wt("v", DS)
                em("v", lambda e: e.tensor_copy(out=Wba[:], in_=Wb32[:]), SV)
                em("v", lambda e: e.tensor_copy(out=thetab[:], in_=stgs[0:64, 0:16]), SV)
                em("v", lambda e: e.tensor_copy(out=tbb[:], in_=stgs[0:1, 16:32]), SV)
                em("v", lambda e: e.tensor_add(out=csml[:, 2:3], in0=csml[:, 0:1], in1=csml[:, 1:2]), SV)
                wt("v", SV)
                em("v", lambda e: e.tensor_scalar(
                    out=csml[:, 3:4], in0=csml[:, 2:3], scalar1=0.0, scalar2=1.0 / HD,
                    op0=OP.max, op1=OP.mult), SV)
                wt("v", DS)
                em("v", lambda e: e.tensor_copy(out=xh1[0:1, :], in_=stgs[0:1, 32:32 + D]), SV)
                
                wt("t", SV)
                for cc in range(2):
                    em("t", lambda e, cc=cc: e.matmul(psA[:, 512 * cc:512 * (cc + 1)], ones1b[0:1, 0:128],
                                                      xh1[0:1, 512 * cc:512 * (cc + 1)], start=True, stop=True), SP, wb=(psA,))
                wt("v", SP)
                em("v", lambda e: e.tensor_copy(out=gBc[:], in_=psA[:]), SV, rp=(psA,))
                wt("s", SV)
                em("s", lambda e: e.dma_start(out=stgs[0:1, 32:32 + D], in_=be_in), DS, 16)
                wt("v", DS)
                em("v", lambda e: e.tensor_copy(out=tmp1[0:1, :], in_=stgs[0:1, 32:32 + D]), SV)
                wt("t", SV)
                for cc in range(2):
                    em("t", lambda e, cc=cc: e.matmul(psA[:, 512 * cc:512 * (cc + 1)], ones1b[0:1, 0:128],
                                                      tmp1[0:1, 512 * cc:512 * (cc + 1)], start=True, stop=True), SP, wb=(psA,))
                wt("v", SP)
                em("v", lambda e: e.tensor_copy(out=betaBc[:], in_=psA[:]), SV, rp=(psA,))
                wt("v", SV)
                em("v", lambda e: e.tensor_mul(out=g2Bc[:], in0=gBc[:], in1=gBc[:]), SV)
                wt("t", SV)
                em("v", lambda e: e.memset(csb[:, 16:32], 0.0), SV)
                em("v", lambda e: e.memset(smb[:, 16:32], 0.0), SV)
                em("v", lambda e: e.memset(smb[:, 48:64], 0.0), SV)

                def proj_group(gi):
                    wt("s", SP)
                    em("s", lambda e, gi=gi: e.dma_start(
                        out=xng[:].rearrange("p (a b) -> p a b", a=8),
                        in_=xnT_sc.rearrange("p (a b) -> p a b", a=8)[:, :, 512 * gi:512 * (gi + 1)]), DS, 16)
                    wt("t", DS)
                    for tb in range(4):
                        it = 4 * gi + tb
                        wt("t", SV)
                        wt("t", SA)
                        for db in range(8):
                            for ci, (pst, wsb) in enumerate(
                                    ((psA, wqb), (psB, wkb), (psC, wvb))):
                                for c in range(2):
                                    em("t", lambda e, tb=tb, db=db, pst=pst, wsb=wsb, c=c: e.matmul(
                                        pst[:, 512 * c:512 * (c + 1)],
                                        xng[:, 512 * db + 128 * tb: 512 * db + 128 * (tb + 1)],
                                        wsb[:, D * db + 512 * c: D * db + 512 * (c + 1)],
                                        start=(db == 0), stop=(db == 7)), SP, wb=(pst,))
                        wt("v", SP)
                        for pst, dst in ((psA, qtok), (psB, ktok), (psC, vtok)):
                            em("v", lambda e, pst=pst, dst=dst, tb=tb: e.tensor_copy(
                                out=dst[:, D * tb:D * (tb + 1)], in_=pst[:]), SV, rp=(pst,))
                        wt("t", SV)
                        for (src, dst) in ((qtok, qT0), (ktok, kT0)):
                            for hh in range(2):
                                for h8 in range(8):
                                    h = 8 * hh + h8
                                    em("t", lambda e, src=src, tb=tb, h=h, h8=h8: e.transpose(
                                        psT[0:64, 128 * h8:128 * (h8 + 1)],
                                        src[:, D * tb + HD * h: D * tb + HD * (h + 1)],
                                        identb[:]), SP, wb=(psT,))
                                wt("v", SP)
                                em("v", lambda e, dst=dst, tb=tb, hh=hh: e.tensor_copy(
                                    out=dst[:].rearrange("p (h t) -> p h t", h=HH)
                                        [:, 8 * hh:8 * (hh + 1), 128 * tb:128 * (tb + 1)],
                                    in_=psT[0:64, :].rearrange("p (h t) -> p h t", h=8)), SV, rp=(psT,))
                                wt("t", SV)
                    pmk["proj"] = mk(SV)

                def wo_group(gi):
                    for dm in range(8):
                        wt("a", SP)
                        em("a", lambda e, dm=dm: e.dma_start(
                            out=wost[:], in_=wo_sc.rearrange("(hb p) m -> p hb m", p=128)
                                [:, :, 128 * dm:128 * (dm + 1)]), DA, 16)
                        wt("t", DA)
                        wt("t", SV)
                        for hb in range(8):
                            em("t", lambda e, dm=dm, hb=hb: e.matmul(
                                psB[:, 0:512],
                                wost[:, 128 * hb:128 * (hb + 1)],
                                outT[:, 512 * hb:512 * (hb + 1)],
                                start=(hb == 0), stop=(hb == 7)), SP, wb=(psB,))
                        wt("v", SP)
                        wt("v", DS)
                        em("v", lambda e: e.tensor_copy(out=woev[:], in_=psB[:, 0:512]), SV, rp=(psB,))
                        wt("s", SV)
                        em("s", lambda e, dm=dm, gi=gi: e.dma_start(
                            out=st_sc[128 * dm:128 * (dm + 1), 512 * gi:512 * (gi + 1)],
                            in_=woev[:]), DS, 16)
                        wt("t", SV)

                pmk = {"proj": 0, "wall": 0, "otv": 0, "e2g": 0}
                pend = {"out": None}

                def flush_out():
                    if pend["out"] is None:
                        return
                    r, cs8, ouv = pend["out"]
                    pend["out"] = None
                    wt("t", SG, ouv)
                    for hb in range(8):
                        em("t", lambda e, r=r, hb=hb: e.transpose(
                            psT[:, 64 * hb:64 * (hb + 1)],
                            tmp1[r:r + 64, 128 * hb:128 * (hb + 1)], identb[r:r + 64, r:r + 64]), SP, wb=(psT,))
                    otv = mk(SP)
                    pmk["otv"] = otv
                    wt("v", SP, otv)
                    em("v", lambda e, cs8=cs8: e.tensor_copy(
                        out=outT[:].rearrange("p (h t) -> p h t", h=8)[:, :, 64 * cs8:64 * (cs8 + 1)],
                        in_=psT[:, 0:512].rearrange("p (h t) -> p h t", h=8)), SV, rp=(psT,))

                def ttt_step(zq, z1):
                    wt("t", SV, pmk["proj"])
                    wt("t", SV, pmk["wall"])
                    for cc in range(2):
                        em("t", lambda e, cc=cc: e.matmul(psA[:, 512 * cc:512 * (cc + 1)], ones1b[64:65, 0:128],
                                                          Wba[64:65, 512 * cc:512 * (cc + 1)], start=True, stop=True), SP, wb=(psA,))
                    em("t", lambda e: e.matmul(psD[:, 0:16], ones1b[0:1, 0:128], tbb[:], start=True, stop=True), SP, wb=(psD,))
                    for h in range(HH):
                        members = [(c, src) for c, src in ((z1, kT0), (zq, qT0)) if c is not None]
                        if "nozq" in p3cut:
                            members = [(c, s_) for c, s_ in members if s_ is kT0]
                        for mi, (c, src) in enumerate(members):
                            r = 64 * (c % 2)
                            tb = (c % 8) // 2
                            em("t", lambda e, h=h, c=c, src=src, r=r, tb=tb: e.matmul(
                                psA[r:r + 64, HD * h:HD * (h + 1)],
                                src[:, 512 * h + 128 * tb + 64 * (c % 2): 512 * h + 128 * tb + 64 * (c % 2) + 64],
                                Wba[0:64, HD * h:HD * (h + 1)],
                                start=False, stop=True, skip_group_check=True,
                                tile_position=(0, r)), SP, wb=(psA,))
                            if src is kT0 and "nolr" not in p3cut:
                                em("t", lambda e, h=h, c=c, r=r, tb=tb: e.matmul(
                                    psD[r:r + 64, h:h + 1],
                                    kT0[:, 512 * h + 128 * tb + 64 * (c % 2): 512 * h + 128 * tb + 64 * (c % 2) + 64],
                                    thetab[:, h:h + 1],
                                    start=False, stop=True, skip_group_check=True,
                                    tile_position=(0, r)), SP, wb=(psD,))
                    zv = mk(SP)
                    flush_out()
                    if "zonly" in p3cut:
                        return
                    if z1 is not None and "nograd" not in p3cut:
                        r1 = 64 * (z1 % 2)
                        tb1 = (z1 % 8) // 2
                        wt("g", SP, pmk["otv"])
                        wt("g", SV, pmk["proj"])
                        em("g", lambda e, r=r1, tb=tb1: e.tensor_sub(
                            out=tmp1[r:r + 64, :], in0=ktok[r:r + 64, D * tb:D * (tb + 1)],
                            in1=vtok[r:r + 64, D * tb:D * (tb + 1)]), SG)
                        wt("g", SG)
                        em("g", lambda e, r=r1: e.tensor_add(
                            out=tmp1[r:r + 64, :], in0=tmp1[r:r + 64, :], in1=betaBc[r:r + 64, :]), SG)
                        wt("g", SG)
                        em("g", lambda e, r=r1: e.tensor_mul(
                            out=e2g[r:r + 64, :], in0=tmp1[r:r + 64, :], in1=gBc[r:r + 64, :]), SG)
                        pmk["e2g"] = mk(SG)
                    wt("v", SP, zv)
                    em("v", lambda e: e.reduce_sum(
                        out=stt[:, 0:16].rearrange("p (s o) -> p s o", o=1),
                        in_=psA[:].rearrange("p (s k) -> p s k", s=HH), axis=AX.X), SV, rp=(psA,))
                    if z1 is not None and "nograd" not in p3cut:
                        rS = 64 * (z1 % 2)
                        wt("a", SP, zv)
                        em("a", lambda e, r=rS: e.activation(
                            out=lrs[r:r + 64, 0:16], in_=psD[r:r + 64, 0:16], func=AF.Sigmoid), SA, rp=(psD,))
                        wt("v", SA)
                        em("v", lambda e, r=rS: e.tensor_scalar_mul(
                            out=lrs[r:r + 64, 16:32], in0=lrs[r:r + 64, 0:16],
                            scalar1=csml[r:r + 64, 3:4]), SV)
                    wt("a", SP, zv)
                    wt("a", SV)
                    em("a", lambda e: e.activation(out=sq_s[:], in_=psA[:], func=AF.Square), SA, rp=(psA,))
                    em("a", lambda e: e.activation(out=wrm[:], in_=wrm[:], func=AF.Sqrt), SA)
                    wt("v", SA)
                    em("v", lambda e: e.reduce_sum(
                        out=stt[:, 16:32].rearrange("p (s o) -> p s o", o=1),
                        in_=sq_s[:].rearrange("p (s k) -> p s k", s=HH), axis=AX.X), SV)
                    wt("v", SV)
                    em("v", lambda e: e.tensor_mul(out=stt[:, 64:80], in0=stt[:, 0:16], in1=stt[:, 0:16]), SV)
                    wt("v", SV)
                    em("v", lambda e: e.scalar_tensor_tensor(
                        out=stt[:, 80:96], in0=stt[:, 16:32], scalar=float(HD), in1=stt[:, 64:80],
                        op0=OP.mult, op1=OP.subtract), SV)
                    sta = mk(SV)
                    wt("a", SV, sta)
                    em("a", lambda e: e.activation(out=stt[:, 96:112], in_=stt[:, 80:96], func=AF.Sqrt,
                                                   bias=epsc3[:], scale=1.0 / (HD * HD)), SA)
                    wt("v", SA)
                    em("v", lambda e: e.reciprocal(out=stt[:, 112:128], in_=stt[:, 96:112]), SV)
                    wt("v", SV)
                    em("v", lambda e: e.scalar_tensor_tensor(
                        out=csb[:, 0:16], in0=stt[:, 0:16], scalar=-1.0 / HD, in1=stt[:, 112:128],
                        op0=OP.mult, op1=OP.mult), SV)
                    em("v", lambda e: e.tensor_copy(out=csb[:, 32:48], in_=stt[:, 112:128]), SV)
                    wt("v", SV)
                    em("v", lambda e: e.tensor_mul(
                        out=xh1[:].rearrange("p (s k) -> p s k", s=HH),
                        in0=psA[:].rearrange("p (s k) -> p s k", s=HH),
                        in1=csb[:, 32:48].rearrange("p (s o) -> p s o", o=1).broadcast_to((128, HH, HD))), SV, rp=(psA,))
                    wt("v", SV)
                    em("v", lambda e: e.tensor_add(
                        out=xhat[:].rearrange("p (s k) -> p s k", s=HH),
                        in0=xh1[:].rearrange("p (s k) -> p s k", s=HH),
                        in1=csb[:, 0:16].rearrange("p (s o) -> p s o", o=1).broadcast_to((128, HH, HD))), SV)
                    xhv = mk(SV)
                    if z1 is not None and "nograd" not in p3cut:
                        r = 64 * (z1 % 2)
                        tb = (z1 % 8) // 2
                        wt("v", SV, xhv)
                        em("v", lambda e, r=r: e.tensor_mul(
                            out=tmp2[r:r + 64, :], in0=xhat[r:r + 64, :], in1=g2Bc[r:r + 64, :]), SV)
                        wt("v", SV)
                        wt("v", SG, pmk["e2g"])
                        em("v", lambda e, r=r: e.tensor_add(
                            out=gp[r:r + 64, 0:1024], in0=tmp2[r:r + 64, :], in1=e2g[r:r + 64, :]), SV)
                        wt("v", SV)
                        em("v", lambda e, r=r: e.tensor_mul(
                            out=gp[r:r + 64, 1024:2048], in0=gp[r:r + 64, 0:1024], in1=xhat[r:r + 64, :]), SV)
                        wt("v", SV)
                        em("v", lambda e, r=r: e.reduce_sum(
                            out=lrs[r:r + 64, 32:64].rearrange("p (s o) -> p s o", o=1),
                            in_=gp[r:r + 64, :].rearrange("p (s k) -> p s k", s=2 * HH), axis=AX.X), SV)
                        wt("v", SV)
                        em("v", lambda e, r=r: e.tensor_mul(
                            out=smb[r:r + 64, 0:16], in0=lrs[r:r + 64, 16:32], in1=stt[r:r + 64, 112:128]), SV)
                        wt("v", SV)
                        em("v", lambda e, r=r: e.scalar_tensor_tensor(
                            out=smb[r:r + 64, 32:48], in0=lrs[r:r + 64, 32:48], scalar=1.0 / HD,
                            in1=smb[r:r + 64, 0:16], op0=OP.mult, op1=OP.mult), SV)
                        em("v", lambda e, r=r: e.scalar_tensor_tensor(
                            out=smb[r:r + 64, 64:80], in0=lrs[r:r + 64, 48:64], scalar=1.0 / HD,
                            in1=smb[r:r + 64, 0:16], op0=OP.mult, op1=OP.mult), SV)
                        smv = mk(SV)
                        wt("v", SV, smv)
                        em("v", lambda e, r=r: e.tensor_mul(
                            out=tmp1[r:r + 64, :].rearrange("p (s k) -> p s k", s=HH),
                            in0=gp[r:r + 64, 0:1024].rearrange("p (s k) -> p s k", s=HH),
                            in1=smb[r:r + 64, 0:16].rearrange("p (s o) -> p s o", o=1).broadcast_to((64, HH, HD))), SV)
                        em("v", lambda e, r=r: e.tensor_mul(
                            out=tmp2[r:r + 64, :].rearrange("p (s k) -> p s k", s=HH),
                            in0=xhat[r:r + 64, :].rearrange("p (s k) -> p s k", s=HH),
                            in1=smb[r:r + 64, 64:80].rearrange("p (s o) -> p s o", o=1).broadcast_to((64, HH, HD))), SV)
                        wt("v", SV)
                        em("v", lambda e, r=r: e.tensor_sub(
                            out=tmp1[r:r + 64, :], in0=tmp1[r:r + 64, :], in1=tmp2[r:r + 64, :]), SV)
                        wt("v", SV)
                        em("v", lambda e, r=r: e.scalar_tensor_tensor(
                            out=gxp[r:r + 64, :].rearrange("p (s k) -> p s k", s=HH),
                            in0=smb[r:r + 64, 32:48].rearrange("p (s o) -> p s o", o=1).broadcast_to((64, HH, HD)),
                            scalar=-1.0,
                            in1=tmp1[r:r + 64, :].rearrange("p (s k) -> p s k", s=HH),
                            op0=OP.mult, op1=OP.add), SV)
                        gxv = mk(SV)
                        wt("t", SV, gxv)
                        for h in range(HH):
                            em("t", lambda e, h=h, r=r, tb=tb: e.matmul(
                                psC[0:64, HD * h:HD * (h + 1)],
                                ktok[r:r + 64, D * tb + HD * h: D * tb + HD * (h + 1)],
                                gxp[r:r + 64, HD * h:HD * (h + 1)],
                                start=True, stop=True), SP, wb=(psC,))
                        for h in range(HH):
                            em("t", lambda e, h=h, r=r: e.matmul(
                                psC[64:65, HD * h:HD * (h + 1)],
                                ones1b[r:r + 64, 0:1],
                                gxp[r:r + 64, HD * h:HD * (h + 1)],
                                start=True, stop=True, skip_group_check=True), SP, wb=(psC,))
                        dwv = mk(SP)
                        wt("v", SP, dwv)
                        em("v", lambda e: e.scalar_tensor_tensor(
                            out=Wba[:], in0=psC[0:65, :], scalar=-1.0, in1=Wb32[:],
                            op0=OP.mult, op1=OP.add), SV, rp=(psC,))
                        pmk["wall"] = mk(SV)
                        em("v", lambda e: e.tensor_sub(out=Wb32[:], in0=Wb32[:], in1=psC[0:65, :]), SV, rp=(psC,))
                    if zq is not None and "noout" not in p3cut:
                        r = 64 * (zq % 2)
                        tb = (zq % 8) // 2
                        cs8 = zq % 8
                        wt("g", SV, xhv)
                        em("g", lambda e, r=r: e.tensor_mul(
                            out=tmp1[r:r + 64, :], in0=xhat[r:r + 64, :], in1=gBc[r:r + 64, :]), SG)
                        wt("g", SG)
                        em("g", lambda e, r=r: e.tensor_add(
                            out=tmp2[r:r + 64, :], in0=tmp1[r:r + 64, :], in1=betaBc[r:r + 64, :]), SG)
                        wt("g", SG)
                        em("g", lambda e, r=r, tb=tb: e.tensor_add(
                            out=tmp1[r:r + 64, :], in0=tmp2[r:r + 64, :],
                            in1=qtok[r:r + 64, D * tb:D * (tb + 1)]), SG)
                        ouv = mk(SG)
                        pend["out"] = (r, cs8, ouv)

                if p3n == 0:
                    pass
                elif p3n == 1:
                    proj_group(0)
                elif p3n == 2:
                    proj_group(0)
                    ttt_step(None, 0)
                elif p3n == 3:
                    proj_group(0)
                    ttt_step(None, 0)
                    ttt_step(0, 1)
                elif p3n == 5:
                    proj_group(0)
                    ttt_step(None, 0)
                    ttt_step(None, 0)
                elif p3n == 4:
                    proj_group(0)
                    ttt_step(None, 0)
                    for c in range(0, 7):
                        ttt_step(c, c + 1)
                    ttt_step(7, None)
                    flush_out()
                    wo_group(0)
                else:
                    for gi in range(NG):
                        proj_group(gi)
                        c0 = 8 * gi
                        ttt_step(None, c0)
                        for c in range(c0, c0 + 7):
                            ttt_step(c, c + 1)
                        ttt_step(c0 + 7, None)
                        flush_out()
                        wo_group(gi)
                barrier()

        if upto < 4:
            raise _Stop
        with phase("p4") as pes:
            def sbp(name, shape, dt=BF16):
                return pes.enter_context(nc.sbuf_tensor(name, list(shape), dt))

            wacc = sbp("wacc", (128, 32 * 512))
            xnf = sbp("xnf", (128, 8 * 512))
            uco = [sbp(f"uco{e}", (64, 512)) for e in range(NE)]
            web = [sbp(f"web{e}", (128, 512)) for e in range(NE)]
            t2b = [sbp(f"t2b{i}", (128, 512)) for i in range(6)]
            mmb = [sbp(f"mmb{i}", (128, 512)) for i in range(6)]
            wab = [sbp(f"wab{i}", (128, 512)) for i in range(6)]
            wgs = [sbp(f"wgs{i}", (128, 3 * 1024)) for i in range(3)]
            wus = [sbp(f"wus{i}", (128, 3 * 1024)) for i in range(3)]
            bt1 = [sbp(f"bt1{i}", (32, 3 * 1024)) for i in range(3)]
            bt3 = [sbp(f"bt3{i}", (64, 3 * 1024)) for i in range(3)]
            a2s = [sbp(f"a2s{i}", (128, 3 * 128)) for i in range(3)]
            wds = [sbp(f"wds{i}", (128, 32 * 128)) for i in range(2)]
            zsb = [sbp(f"zsb{g}", (128, 512)) for g in range(2)]
            dnev = [sbp(f"dnev{i}", (128, 512), F32) for i in range(2)]
            zrow = sbp("zrow", (1, 128))
            em("v", lambda e: e.memset(zrow[:], 0.0), SV)
            for gz in range(2):
                em("v", lambda e, gz=gz: e.memset(zsb[gz][:], 0.0), SV)
            psD1 = [pes.enter_context(nc.psum_tensor(f"psD1{i}", [128, 512], F32)) for i in range(3)]
            psD3 = [pes.enter_context(nc.psum_tensor(f"psD3{i}", [128, 512], F32)) for i in range(3)]
            psZ = [pes.enter_context(nc.psum_tensor(f"psZ{g}", [128, 512], F32)) for g in range(2)]
            psM = psD1
            DWq = [DW, DW1, DW2]
            DDq = [DD0, DD1]
            DGq = [DGa, DGb]

            NJ = 32 if p4n < 0 else p4n
            NTC = T // 512

            def qload(gt, gate):
                sl = gt % 3
                kk = gt % 11
                if gate is not None:
                    wt("s", SP, gate)
                for j in [j_ for j_ in (3 * kk, 3 * kk + 1, 3 * kk + 2) if j_ < 32]:
                    jj = j % 3
                    co = 1024 * jj
                    em("s", lambda e, j=j, sl=sl, co=co: e.dma_start(
                        out=wgs[sl][:, co:co + 1024].rearrange("p (a b) -> p a b", a=8),
                        in_=wg_sc.rearrange("(a p) f -> p a f", p=128)[:, :, 128 * j:128 * (j + 1)]), DWq[sl], 16)
                    em("s", lambda e, j=j, sl=sl, co=co: e.dma_start(
                        out=wus[sl][:, co:co + 1024].rearrange("p (a b) -> p a b", a=8),
                        in_=wu_sc.rearrange("(a p) f -> p a f", p=128)[:, :, 128 * j:128 * (j + 1)]), DWq[sl], 16)
                    em("s", lambda e, j=j, sl=sl, co=co: e.dma_start(
                        out=bt1[sl][0:16, co:co + 1024].rearrange("p (a b) -> p a b", a=8),
                        in_=b1pos_sc.rearrange("p (a f) -> p a f", a=8)[:, :, 128 * j:128 * (j + 1)]), DWq[sl], 16)
                    em("s", lambda e, j=j, sl=sl, co=co: e.dma_start(
                        out=bt1[sl][16:32, co + 128:co + 1024].rearrange("p (a b) -> p a b", a=7),
                        in_=b1pos_sc.rearrange("p (a f) -> p a f", a=8)[:, 0:7, 128 * j:128 * (j + 1)]), DWq[sl], 16)
                    em("s", lambda e, j=j, sl=sl, co=co: e.dma_start(
                        out=bt3[sl][32:48, co:co + 1024].rearrange("p (a b) -> p a b", a=8),
                        in_=b3pos_sc.rearrange("p (a f) -> p a f", a=8)[:, :, 128 * j:128 * (j + 1)]), DWq[sl], 16)
                    em("s", lambda e, j=j, sl=sl, co=co: e.dma_start(
                        out=bt3[sl][48:64, co + 128:co + 1024].rearrange("p (a b) -> p a b", a=7),
                        in_=b3pos_sc.rearrange("p (a f) -> p a f", a=8)[:, 0:7, 128 * j:128 * (j + 1)]), DWq[sl], 16)
                    em("s", lambda e, j=j, sl=sl, jj=jj: e.dma_start(
                        out=a2s[sl][:, 128 * jj:128 * (jj + 1)],
                        in_=a2_sc[128 * j:128 * (j + 1), :]), DWq[sl], 16)

            def up_mm(j, tc):
                sl, jj = (11 * tc + j // 3) % 3, j % 3
                pj = j % 3
                wt("t", DWq[sl])
                for db in range(8):
                    em("t", lambda e, pj=pj, sl=sl, jj=jj, db=db: e.matmul(
                        psD1[pj][:, :], wgs[sl][:, 1024 * jj + 128 * db:1024 * jj + 128 * (db + 1)],
                        xnf[:, 512 * db:512 * (db + 1)],
                        start=(db == 0), stop=(db == 7)), SP, wb=(psD1[pj],))
                for db in range(8):
                    em("t", lambda e, pj=pj, sl=sl, jj=jj, db=db: e.matmul(
                        psD3[pj][:, :], wus[sl][:, 1024 * jj + 128 * db:1024 * jj + 128 * (db + 1)],
                        xnf[:, 512 * db:512 * (db + 1)],
                        start=(db == 0), stop=(db == 7)), SP, wb=(psD3[pj],))

            def chain_triple(js, tc):
                tm1 = {}
                tm3 = {}
                am = {}

                def zgroup(e_):
                    for j in js:
                        sl, jj = (11 * tc + j // 3) % 3, j % 3
                        bi = 2 * (j % 3) + (e_ % 2)
                        wt("t", SV)
                        em("t", lambda e, e_=e_, bi=bi, sl=sl, jj=jj: e.matmul(
                            psZ[e_ // 4][32 * (e_ % 4):32 * (e_ % 4) + 16, :],
                            a2s[sl][:, 128 * jj + 16 * e_:128 * jj + 16 * (e_ + 1)], wab[bi][:],
                            start=False, stop=True,
                            skip_group_check=True,
                            tile_position=(0, 32 * (e_ % 4))), SP, wb=(psZ[e_ // 4],))
                        zlast[bi] = mk(SP)

                for e_ in range(NE):
                    for j in js:
                        sl, jj = (11 * tc + j // 3) % 3, j % 3
                        pj = j % 3
                        if e_ == 0:
                            em("t", lambda e, pj=pj, sl=sl, jj=jj: e.matmul(
                                psD1[pj][:, :], bt1[sl][0:16, 1024 * jj:1024 * jj + 128], uco[0][0:16, :],
                                start=False, stop=True, skip_group_check=True), SP, wb=(psD1[pj],))
                        else:
                            em("t", lambda e, pj=pj, sl=sl, jj=jj, e_=e_: e.matmul(
                                psD1[pj][:, :], bt1[sl][0:32, 1024 * jj + 128 * e_:1024 * jj + 128 * (e_ + 1)],
                                uco[e_][0:32, :], start=False, stop=True, skip_group_check=True), SP, wb=(psD1[pj],))
                        tm1[j] = mk(SP)
                    for j in js:
                        sl, jj = (11 * tc + j // 3) % 3, j % 3
                        pj = j % 3
                        if e_ == 0:
                            em("t", lambda e, pj=pj, sl=sl, jj=jj: e.matmul(
                                psD3[pj][:, :], bt3[sl][32:48, 1024 * jj:1024 * jj + 128], uco[0][32:48, :],
                                start=False, stop=True, skip_group_check=True,
                                tile_position=(32, 0)), SP, wb=(psD3[pj],))
                        else:
                            em("t", lambda e, pj=pj, sl=sl, jj=jj, e_=e_: e.matmul(
                                psD3[pj][:, :], bt3[sl][32:64, 1024 * jj + 128 * e_:1024 * jj + 128 * (e_ + 1)],
                                uco[e_][32:64, :], start=False, stop=True, skip_group_check=True,
                                tile_position=(32, 0)), SP, wb=(psD3[pj],))
                        tm3[j] = mk(SP)
                    if e_ > 0:
                        zgroup(e_ - 1)
                    for j in js:
                        pj = j % 3
                        bi = 2 * (j % 3) + (e_ % 2)
                        wt("a", SP, tm1[j])
                        pv = vlast.get(bi)
                        if pv is not None:
                            wt("a", SV, pv)
                        em("a", lambda e, pj=pj, bi=bi: e.activation(
                            out=t2b[bi][:], in_=psD1[pj][:], func=AF.Silu), SA, rp=(psD1[pj],))
                        am[j] = mk(SA)
                    for j in js:
                        pj = j % 3
                        bi = 2 * (j % 3) + (e_ % 2)
                        wt("v", SA, am[j])
                        wt("v", SP, tm3[j])
                        em("v", lambda e, pj=pj, bi=bi: e.tensor_mul(
                            out=mmb[bi][:], in0=t2b[bi][:], in1=psD3[pj][:]), SV, rp=(psD3[pj],))
                        vlast[bi] = mk(SV)
                    for j in js:
                        bi = 2 * (j % 3) + (e_ % 2)
                        wt("v", SV)
                        zl = zlast.get(bi)
                        if zl is not None:
                            wt("v", SP, zl)
                        gl = glast.get(bi)
                        if gl is not None:
                            if gl[0] == "g":
                                wt("v", SG, gl[1])
                            else:
                                wt("v", DAC)
                        if e_ == 0:
                            em("v", lambda e, j=j, bi=bi: e.tensor_mul(
                                out=wacc[:, 512 * j:512 * (j + 1)], in0=mmb[bi][:], in1=web[0][:]), SV)
                            wt("v", SV)
                            em("v", lambda e, j=j, bi=bi: e.tensor_copy(
                                out=wab[bi][:], in_=wacc[:, 512 * j:512 * (j + 1)]), SV)
                        else:
                            em("v", lambda e, e_=e_, bi=bi: e.tensor_mul(
                                out=wab[bi][:], in0=mmb[bi][:], in1=web[e_][:]), SV)
                            wt("g", SV)
                            pj1 = gj.get(j)
                            if pj1 is not None:
                                if pj1[0] == "g":
                                    wt("g", SG, pj1[1])
                                else:
                                    wt("g", DAC)
                            if e_ % 2 == 1:
                                em("g", lambda e, j=j, bi=bi: e.tensor_add(
                                    out=wacc[:, 512 * j:512 * (j + 1)],
                                    in0=wacc[:, 512 * j:512 * (j + 1)], in1=wab[bi][:]), SG)
                                gj[j] = glast[bi] = ("g", mk(SG))
                            else:
                                em("g", lambda e, j=j, bi=bi: e.dma_start(
                                    out=wacc[:, 512 * j:512 * (j + 1)],
                                    in_=wab[bi][:], accum_op=OP.add), DAC, 16)
                                gj[j] = glast[bi] = ("d", DAC.n)
                zgroup(NE - 1)

            qload(0, None)
            qload(1, None)
            qload(2, None)
            dnm = {}
            dgn = {}
            trimark = {}
            zlast = {}
            glast = {}
            vlast = {}
            gj = {}

            for tc in range(NTC):
                if tc == 0:
                    em("s", lambda e, tc=tc: e.dma_start(
                        out=xnf[:].rearrange("p (a b) -> p a b", a=8),
                        in_=xnT_sc.rearrange("p (a b) -> p a b", a=8)[:, :, 512 * tc:512 * (tc + 1)]), DS, 16)
                wt("t", DS)
                wt("t", SV)
                for gz in range(2):
                    em("t", lambda e, gz=gz: e.matmul(
                        psZ[gz][:, :], zrow[:], wTb[0:1, 0:512], start=True, stop=True), SP, wb=(psZ[gz],))
                for e_ in range(NE):
                    wt("t", SV)
                    for db in range(8):
                        em("t", lambda e, e_=e_, db=db, tc=tc: e.matmul(
                            psM[e_ % 2][0:64, :],
                            acb[:, 512 * db + 64 * e_: 512 * db + 64 * (e_ + 1)],
                            xnf[:, 512 * db:512 * (db + 1)],
                            start=(db == 0), stop=(db == 7)), SP, wb=(psM[e_ % 2],))
                    wt("v", SP)
                    em("v", lambda e, e_=e_: e.tensor_copy(out=uco[e_][:], in_=psM[e_ % 2][0:64, :]), SV, rp=(psM[e_ % 2],))
                for e_ in range(NE):
                    wt("t", SV)
                    em("t", lambda e, e_=e_, tc=tc: e.matmul(
                        psM[e_ % 2][:, :], pat8b[:, 128 * e_:128 * (e_ + 1)],
                        wTb[:, 512 * tc:512 * (tc + 1)], start=True, stop=True), SP, wb=(psM[e_ % 2],))
                    wt("v", SP)
                    em("v", lambda e, e_=e_: e.tensor_copy(out=web[e_][:], in_=psM[e_ % 2][:]), SV, rp=(psM[e_ % 2],))

                gj.clear()
                for j in range(min(3, NJ)):
                    up_mm(j, tc)
                ktc = (NJ + 2) // 3
                for k in range(ktc):
                    js = [j for j in (3 * k, 3 * k + 1, 3 * k + 2) if j < NJ]
                    chain_triple(js, tc)
                    trimark[(tc, k)] = mk(SP)
                    gt = 11 * tc + k
                    if p4n < 0 and gt + 3 < 11 * NTC:
                        qload(gt + 3, trimark[(tc, k)])
                    for j in (3 * k + 3, 3 * k + 4, 3 * k + 5):
                        if j < NJ:
                            up_mm(j, tc)
                if p4n >= 0:
                    barrier()
                    raise _Stop
                if tc + 1 < NTC:
                    wt("s", SP, mk(SP))
                    em("s", lambda e, tc=tc: e.dma_start(
                        out=xnf[:].rearrange("p (a b) -> p a b", a=8),
                        in_=xnT_sc.rearrange("p (a b) -> p a b", a=8)[:, :, 512 * (tc + 1):512 * (tc + 2)]), DS, 16)
                wt("t", SG)
                wt("t", DAC)
                wt("v", SP)
                for gz in range(2):
                    for jj in range(4):
                        em("v", lambda e, gz=gz, jj=jj: e.tensor_copy(
                            out=zsb[gz][32 * jj:32 * jj + 16, :],
                            in_=psZ[gz][32 * jj:32 * jj + 16, :]), SV, rp=(psZ[gz],))

                def wdload(dm, gate):
                    sl = dm % 2
                    if gate is not None:
                        wt("s", SP, gate)
                    em("s", lambda e, dm=dm, sl=sl: e.dma_start(
                        out=wds[sl][:].rearrange("p (a b) -> p a b", a=32),
                        in_=wd_sc.rearrange("(a p) m -> p a m", p=128)[:, :, 128 * dm:128 * (dm + 1)]), DDq[sl], 16)

                wdload(0, dnm.get((tc - 1, 6)))
                wdload(1, dnm.get((tc - 1, 7)))
                for dm in range(8):
                    pd = dm % 2
                    wt("t", DDq[pd])
                    wt("t", SV)
                    for j in range(32):
                        em("t", lambda e, j=j, pd=pd: e.matmul(
                            psM[pd][:, :], wds[pd][:, 128 * j:128 * (j + 1)],
                            wacc[:, 512 * j:512 * (j + 1)], start=(j == 0), stop=(j == 31)), SP, wb=(psM[pd],))
                    for gz in range(2):
                        em("t", lambda e, gz=gz, dm=dm, pd=pd: e.matmul(
                            psM[pd][:, :], b2s[gz][:, 128 * dm:128 * (dm + 1)], zsb[gz][:],
                            start=False, stop=True, skip_group_check=True), SP, wb=(psM[pd],))
                    dnm[(tc, dm)] = mk(SP)
                    if dm + 2 < 8:
                        wdload(dm + 2, dnm[(tc, dm)])
                    wt("v", SP)
                    wt("v", DGq[pd])
                    em("v", lambda e, pd=pd: e.tensor_copy(out=dnev[pd][:], in_=psM[pd][:]), SV, rp=(psM[pd],))
                    wt("g", SV)
                    em("g", lambda e, dm=dm, tc=tc, pd=pd: e.dma_start(
                        out=st_sc[128 * dm:128 * (dm + 1), 512 * tc:512 * (tc + 1)],
                        in_=dnev[pd][:], accum_op=OP.add), DGq[pd], 16)
            barrier()

        if upto < 5:
            raise _Stop
        with phase("p5") as pes:
            fx = pes.enter_context(nc.sbuf_tensor("fx", [128, D], F32))
            fst = pes.enter_context(nc.sbuf_tensor("fst", [128, D], F32))
            fo = pes.enter_context(nc.sbuf_tensor("fo", [128, D], F32))
            identf = pes.enter_context(nc.sbuf_tensor("identf", [128, 128], F32))
            psF = pes.enter_context(nc.psum_tensor("psF", [128, 1024], F32))

            em("s", lambda e: e.dma_start(out=identf[:], in_=id_in), DS, 16)
            if debug:
                em("s", lambda e: e.dma_start(out=dbg["st"], in_=st_sc), DS, 16)
            for it in range(NT):
                wt("s", SV)
                em("s", lambda e, it=it: e.dma_start(
                    out=fx[:], in_=x_in[it * 128:(it + 1) * 128, :]), DS, 16)
                em("s", lambda e, it=it: e.dma_start(
                    out=fst[:].rearrange("p (a b) -> p a b", a=8),
                    in_=st_sc.rearrange("(a p) t -> p a t", p=128)[:, :, 128 * it:128 * (it + 1)]), DS, 16)
                wt("t", DS)
                wt("t", SV)
                for db in range(8):
                    em("t", lambda e, db=db: e.transpose(
                        psF[:, 128 * db:128 * (db + 1)], fst[:, 128 * db:128 * (db + 1)],
                        identf[:]), SP, wb=(psF,))
                wt("v", SP)
                wt("v", DS)
                em("v", lambda e: e.tensor_add(out=fo[:], in0=fx[:], in1=psF[:]), SV, rp=(psF,))
                wt("s", SV)
                em("s", lambda e, it=it: e.dma_start(
                    out=y_out[it * 128:(it + 1) * 128, :], in_=fo[:]), DS, 16)
            for k in EK:
                wtall(k)
    except _Stop:
        pass

    return nc


def make_inputs(full, core):
    b = core
    d = {}
    d["x"] = np.ascontiguousarray(full["L0_attn_hidden_states"][b])
    d["wq"] = np.asarray(full["L0_attn_wq"])
    d["wk"] = np.asarray(full["L0_attn_wk"])
    d["wv"] = np.asarray(full["L0_attn_wv"])
    d["wo"] = np.asarray(full["L0_attn_wo"])
    W1 = np.asarray(full["L0_attn_W1"])
    d["w1s"] = np.ascontiguousarray(W1.transpose(1, 0, 2).reshape(HD, HH * HD))
    d["b1s"] = np.ascontiguousarray(np.asarray(full["L0_attn_b1"]).reshape(1, HH * HD))
    d["thetas"] = np.ascontiguousarray(np.asarray(full["L0_attn_theta"])[:, :, 0].T)
    d["tbs"] = np.ascontiguousarray(np.asarray(full["L0_attn_theta_bias"]).reshape(1, HH))
    d["alphas"] = np.ascontiguousarray(np.asarray(full["L0_attn_alpha"]).reshape(CH, 1))
    d["gammas"] = np.ascontiguousarray(np.asarray(full["L0_attn_gamma"]).reshape(1, HH * HD))
    d["betas"] = np.ascontiguousarray(np.asarray(full["L0_attn_beta"]).reshape(1, HH * HD))
    d["gwt"] = np.ascontiguousarray(np.asarray(full["L0_ffn_gate_w"]).T)
    d["wgt"] = np.ascontiguousarray(np.asarray(full["L0_ffn_w_gate"]).T)
    d["wut"] = np.ascontiguousarray(np.asarray(full["L0_ffn_w_up"]).T)
    d["wdt"] = np.ascontiguousarray(np.asarray(full["L0_ffn_w_down"]).T)
    A1 = np.asarray(full["L0_ffn_A1"]); A3 = np.asarray(full["L0_ffn_A3"])
    ac = np.zeros((D, NE * 64), np.float32)
    for e in range(NE):
        ac[:, 64 * e + 0:64 * e + 16] = A1[e].T
        if e > 0:
            ac[:, 64 * e + 16:64 * e + 32] = A1[e - 1].T
        ac[:, 64 * e + 32:64 * e + 48] = A3[e].T
        if e > 0:
            ac[:, 64 * e + 48:64 * e + 64] = A3[e - 1].T
    d["acomb"] = ac
    B1 = np.asarray(full["L0_ffn_B1"]); B3 = np.asarray(full["L0_ffn_B3"])
    d["b1p"] = np.ascontiguousarray(B1.transpose(2, 0, 1).reshape(NR, NE * FF))
    d["b3p"] = np.ascontiguousarray(B3.transpose(2, 0, 1).reshape(NR, NE * FF))
    A2 = np.asarray(full["L0_ffn_A2"])
    d["a2sc"] = np.ascontiguousarray(A2.transpose(2, 0, 1).reshape(FF, NE * NR))
    B2 = np.asarray(full["L0_ffn_B2"])
    b2tp = np.zeros((2 * 128, D), np.float32)
    for e in range(NE):
        g = e // 4
        jj = e % 4
        b2tp[128 * g + 32 * jj:128 * g + 32 * jj + NR, :] = B2[e].T
    d["b2tp"] = b2tp
    d["identc"] = np.eye(128, dtype=np.float32)
    d["onesc"] = np.ones((128, 128), np.float32)
    d["invkc"] = (1.0 / (np.arange(128) % CH + 1.0)).astype(np.float32).reshape(128, 1)
    p8 = np.zeros((NE, NE * 128), np.float32)
    for e in range(NE):
        p8[e, 128 * e:128 * (e + 1)] = 1.0
    d["pat8c"] = p8
    p16 = np.zeros((HH, HH * HD), np.float32)
    for h in range(HH):
        p16[h, HD * h:HD * (h + 1)] = 1.0
    d["pat16c"] = np.tile(p16, (5, 1))
    return d


def slice_T(d, T):
    d = dict(d)
    d["x"] = d["x"][:T]
    return d


def kernel(**inputs):
    nc = build(T=4096)
    in_maps = [make_inputs(inputs, b) for b in range(NCORES)]
    res = run_bass_kernel_spmd(nc, in_maps, core_ids=list(range(NCORES)))
    out = np.stack([np.asarray(res.results[b]["y"]) for b in range(NCORES)], axis=0)
    return out.astype(np.float32)


# version 17 (final)
# speedup vs baseline: 1.1515x; 1.1515x over previous
import numpy as np
from contextlib import ExitStack

import concourse.bass as bass
import concourse.mybir as mybir
from concourse.bass_utils import run_bass_kernel_spmd

F32 = mybir.dt.float32
BF16 = mybir.dt.bfloat16
OP = mybir.AluOpType
AF = mybir.ActivationFunctionType
AX = mybir.AxisListType

D = 1024
HH = 16
HD = 64
CH = 64
FF = 4096
NE = 8
NR = 16
NCORES = 8


class Sem:
    def __init__(self, name, dma=False):
        self.name = name
        self.n = 0
        self.h = None
        self.dma = dma
        self.last_unmarked = None

    def mark(self):
        if self.last_unmarked is not None:
            self.last_unmarked[2][0] = True
            self.n += self.last_unmarked[1]
            self.last_unmarked = None


def build(T=4096, debug=False, upto=9, p3n=-1, p3cut="", sparse=True, p4n=-1):
    nc = bass.Bass("TRN2", target_bir_lowering=False)
    NT = T // 128
    NG = T // 512
    NCH = T // CH

    def inp(name, shape):
        return nc.dram_tensor(name, list(shape), F32, kind="ExternalInput").ap()

    x_in = inp("x", (T, D))
    wq_in = inp("wq", (D, D))
    wk_in = inp("wk", (D, D))
    wv_in = inp("wv", (D, D))
    wo_in = inp("wo", (D, D))
    w1_in = inp("w1s", (HD, HH * HD))
    b1_in = inp("b1s", (1, HH * HD))
    th_in = inp("thetas", (HD, HH))
    tb_in = inp("tbs", (1, HH))
    al_in = inp("alphas", (CH, 1))
    ga_in = inp("gammas", (1, HH * HD))
    be_in = inp("betas", (1, HH * HD))
    gw_in = inp("gwt", (D, NE))
    wg_in = inp("wgt", (D, FF))
    wu_in = inp("wut", (D, FF))
    wd_in = inp("wdt", (FF, D))
    ac_in = inp("acomb", (D, NE * 64))
    b1p_in = inp("b1p", (NR, NE * FF))
    b3p_in = inp("b3p", (NR, NE * FF))
    a2_in = inp("a2sc", (FF, NE * NR))
    b2_in = inp("b2tp", (2 * 128, D))
    id_in = inp("identc", (128, 128))
    ones_in = inp("onesc", (128, 128))
    inv_in = inp("invkc", (128, 1))
    pat8_in = inp("pat8c", (NE, NE * 128))
    pat16_in = inp("pat16c", (80, HH * HD))

    y_out = nc.dram_tensor("y", [T, D], F32, kind="ExternalOutput").ap()
    dbg = {}
    if debug:
        dbg["xn"] = nc.dram_tensor("dbg_xn", [T, D], F32, kind="ExternalOutput").ap()
        dbg["w"] = nc.dram_tensor("dbg_w", [T, NE], F32, kind="ExternalOutput").ap()
        dbg["st"] = nc.dram_tensor("dbg_st", [D, T], F32, kind="ExternalOutput").ap()

    def scr(name, shape, dt=BF16):
        return nc.dram_tensor(name, list(shape), dt, kind="Internal").ap()

    wg_sc = scr("wg_sc", (D, FF))
    wu_sc = scr("wu_sc", (D, FF))
    wd_sc = scr("wd_sc", (FF, D))
    wo_sc = scr("wo_sc", (D, D))
    wq_sc = scr("wq_sc", (D, D))
    wk_sc = scr("wk_sc", (D, D))
    wv_sc = scr("wv_sc", (D, D))
    a2_sc = scr("a2_sc", (FF, NE * NR))
    b1pos_sc = scr("b1pos_sc", (NR, NE * FF))
    b3pos_sc = scr("b3pos_sc", (NR, NE * FF))
    st_sc = scr("st_sc", (D, T), F32)
    xnT_sc = scr("xnT_sc", (128, 8 * T))

    SP = Sem("sp"); SV = Sem("sv"); SA = Sem("sa"); SG = Sem("sg")
    DS = Sem("ds", dma=True); DA = Sem("da", dma=True); DG = Sem("dg", dma=True)
    DW = Sem("dw", dma=True); DW1 = Sem("dw1", dma=True); DW2 = Sem("dw2", dma=True)
    DD0 = Sem("dd0", dma=True); DD1 = Sem("dd1", dma=True)
    DGa = Sem("dga", dma=True); DGb = Sem("dgb", dma=True); DAC = Sem("dac", dma=True)
    SEMS = [SP, SV, SA, SG, DS, DA, DG, DW, DW1, DW2, DD0, DD1, DGa, DGb, DAC]
    EK = ("t", "v", "a", "g", "s")

    cur = {"plan": None, "lastw": None}
    prd = {}

    def wt(key, sem, val=None):
        if val is None:
            sem.mark()
            v = sem.n
        else:
            v = val
        if v <= 0:
            return
        lw = cur["lastw"][key]
        if lw.get(sem.name, -1) >= v:
            return
        lw[sem.name] = v
        cur["plan"][key].append(lambda e, s=sem, v=v: e.wait_ge(s.h, v))

    def em(key, fn, sem=None, k=1, wb=(), rp=()):
        if key == "t":
            for t_ in wb:
                info = prd.get(t_.name)
                if info:
                    if info["v"]:
                        wt("t", SV, info["v"])
                    if info["a"]:
                        wt("t", SA, info["a"])
        elif key in ("v", "a") and rp:
            other = "a" if key == "v" else "v"
            osem = SA if key == "v" else SV
            for t_ in rp:
                info = prd.get(t_.name)
                if info and info[other]:
                    wt(key, osem, info[other])
        if sem is None:
            cur["plan"][key].append(fn)
        elif sem.dma or not sparse:
            if not sem.dma:
                sem.mark()
                sem.n += k
                cur["plan"][key].append(lambda e, f=fn, s=sem, k=k: f(e).then_inc(s.h, k))
            else:
                sem.n += k
                cur["plan"][key].append(lambda e, f=fn, s=sem, k=k: f(e).then_inc(s.h, k))
        else:
            ent = [fn, k, [False]]
            sem.last_unmarked = ent
            cur["plan"][key].append(
                lambda e, f=fn, s=sem, k=k, fl=ent[2]: (f(e).then_inc(s.h, k) if fl[0] else f(e)))
        if key in ("v", "a") and rp and sem is not None:
            sem.mark()
            for t_ in rp:
                info = prd.setdefault(t_.name, {"v": 0, "a": 0})
                info[key] = sem.n

    def wtall(key):
        for s_ in SEMS:
            wt(key, s_)

    def barrier():
        for k in EK:
            wtall(k)

    def mk(sem):
        sem.mark()
        return sem.n

    class phase:
        def __init__(self, name):
            self.name = name

        def __enter__(self):
            self.es = ExitStack()
            self.es.__enter__()
            cur["plan"] = {k: [] for k in EK}
            cur["lastw"] = {k: {} for k in EK}
            return self.es

        def __exit__(self, *a):
            for s_ in SEMS:
                if not s_.dma:
                    s_.mark()
            plan = cur["plan"]
            with nc.Block() as block:
                block.tensor(lambda e: [f(e) for f in plan["t"]])
                block.vector(lambda e: [f(e) for f in plan["v"]])
                block.scalar(lambda e: [f(e) for f in plan["a"]])
                block.gpsimd(lambda e: [f(e) for f in plan["g"]])
                block.sync(lambda e: [f(e) for f in plan["s"]])
            cur["plan"] = None
            return self.es.__exit__(*a)

    class _Stop(Exception):
        pass

    es0 = ExitStack()
    try:
      with es0:
        def sb0(name, shape, dt=BF16):
            return es0.enter_context(nc.sbuf_tensor(name, list(shape), dt))

        for s_ in SEMS:
            s_.h = es0.enter_context(nc.semaphore(s_.name))
        wTb = sb0("wTb", (NE, T))
        identb = sb0("identb", (128, 128))
        ones1b = sb0("ones1b", (128, 128))
        acb = sb0("acb", (128, 8 * 512))
        pat8b = sb0("pat8b", (NE, NE * 128))
        pat16b = sb0("pat16b", (80, HH * HD))
        b2s = [sb0("b2s0", (128, D)), sb0("b2s1", (128, D))]

        mid = ExitStack()
        with mid:
            wqb = mid.enter_context(nc.sbuf_tensor("wqb", [128, 8 * D], BF16))
            wkb = mid.enter_context(nc.sbuf_tensor("wkb", [128, 8 * D], BF16))
            wvb = mid.enter_context(nc.sbuf_tensor("wvb", [128, 8 * D], BF16))
            attn_w_loaded = [False]

            with phase("p0") as pes:
                stg32 = pes.enter_context(nc.sbuf_tensor("stg32", [128, 2048], F32))
                stgbf = pes.enter_context(nc.sbuf_tensor("stgbf", [128, 2048], BF16))

                def conv(src_ap, rows, cols, dst_sb=None, dst_dr=None, scale=None,
                         dst_sb_colw=None):
                    for r0 in range(0, rows, 128):
                        rr = min(128, rows - r0)
                        for c0 in range(0, cols, 2048):
                            cc = min(2048, cols - c0)
                            wt("a", SG)
                            wt("a", SV)
                            em("a", lambda e, r0=r0, c0=c0, rr=rr, cc=cc: e.dma_start(
                                out=stg32[0:rr, 0:cc], in_=src_ap[r0:r0 + rr, c0:c0 + cc]), DA, 16)
                            wt("g", DA)
                            if scale is None:
                                em("g", lambda e, rr=rr, cc=cc: e.tensor_copy(
                                    out=stgbf[0:rr, 0:cc], in_=stg32[0:rr, 0:cc]), SG)
                            else:
                                em("g", lambda e, rr=rr, cc=cc, sc=scale: e.tensor_scalar_mul(
                                    out=stgbf[0:rr, 0:cc], in0=stg32[0:rr, 0:cc], scalar1=sc), SG)
                            if dst_dr is not None:
                                wt("a", SG)
                                em("a", lambda e, r0=r0, c0=c0, rr=rr, cc=cc: e.dma_start(
                                    out=dst_dr[r0:r0 + rr, c0:c0 + cc], in_=stgbf[0:rr, 0:cc]), DA, 16)
                            else:
                                wt("v", SG)
                                cw = cols if dst_sb_colw is None else dst_sb_colw
                                em("v", lambda e, r0=r0, c0=c0, rr=rr, cc=cc, cw=cw: e.tensor_copy(
                                    out=dst_sb[0:rr, (r0 // 128) * cw + c0:(r0 // 128) * cw + c0 + cc],
                                    in_=stgbf[0:rr, 0:cc]), SV)

                em("s", lambda e: e.dma_start(out=stg32[:, 0:128], in_=id_in), DS, 16)
                em("s", lambda e: e.dma_start(out=stg32[:, 128:256], in_=ones_in), DS, 16)
                em("s", lambda e: e.dma_start(out=stg32[0:NE, 256:256 + NE * 128], in_=pat8_in), DS, 16)
                wt("v", DS)
                em("v", lambda e: e.tensor_copy(out=identb[:], in_=stg32[:, 0:128]), SV)
                em("v", lambda e: e.tensor_copy(out=ones1b[:], in_=stg32[:, 128:256]), SV)
                em("v", lambda e: e.tensor_copy(out=pat8b[:], in_=stg32[0:NE, 256:256 + NE * 128]), SV)
                wt("s", SV)
                em("s", lambda e: e.dma_start(out=stg32[0:80, 0:1024], in_=pat16_in), DS, 16)
                wt("v", DS)
                em("v", lambda e: e.tensor_copy(out=pat16b[:], in_=stg32[0:80, 0:1024]), SV)

                for s_ap, d_ap in ((wq_in, wq_sc), (wk_in, wk_sc), (wv_in, wv_sc),
                                   (wo_in, wo_sc), (wg_in, wg_sc), (wu_in, wu_sc),
                                   (wd_in, wd_sc), (a2_in, a2_sc),
                                   (b1p_in, b1pos_sc), (b3p_in, b3pos_sc)):
                    em("g", lambda e, s_ap=s_ap, d_ap=d_ap: e.dma_start(out=d_ap, in_=s_ap), DG, 16)
                for r0 in range(0, D, 128):
                    wt("a", SV)
                    em("a", lambda e, r0=r0: e.dma_start(
                        out=stg32[:, 0:512], in_=ac_in[r0:r0 + 128, :]), DA, 16)
                    wt("v", DA)
                    vv = lambda t: t.rearrange("p (e a b r) -> p e a b r", e=8, a=2, b=2)
                    em("v", lambda e, vv=vv: e.tensor_scalar_mul(
                        out=vv(stgbf[:, 0:512])[:, :, :, 0:1, :],
                        in0=vv(stg32[:, 0:512])[:, :, :, 0:1, :], scalar1=2.0), SV)
                    wt("v", SV)
                    em("v", lambda e, vv=vv: e.tensor_scalar_mul(
                        out=vv(stgbf[:, 0:512])[:, :, :, 1:2, :],
                        in0=vv(stg32[:, 0:512])[:, :, :, 1:2, :], scalar1=-2.0), SV)
                    wt("v", SV)
                    em("v", lambda e, r0=r0: e.tensor_copy(
                        out=acb[:, (r0 // 128) * 512:(r0 // 128) * 512 + 512],
                        in_=stgbf[:, 0:512]), SV)
                conv(b2_in[0:128, :], 128, D, dst_sb=b2s[0], scale=2.0)
                conv(b2_in[128:256, :], 128, D, dst_sb=b2s[1], scale=2.0)
                barrier()

            if upto < 1:
                raise _Stop
            with phase("p1") as pes:
                lnx = [pes.enter_context(nc.sbuf_tensor(f"lnx{i}", [128, D], F32)) for i in range(2)]
                lnsq = [pes.enter_context(nc.sbuf_tensor(f"lnsq{i}", [128, D], F32)) for i in range(2)]
                lns = [pes.enter_context(nc.sbuf_tensor(f"lns{i}", [128, 16], F32)) for i in range(2)]
                lnxn = pes.enter_context(nc.sbuf_tensor("lnxn", [128, D], BF16))
                xout = [pes.enter_context(nc.sbuf_tensor(f"xout{i}", [128, D], BF16)) for i in range(2)]
                xn32 = [pes.enter_context(nc.sbuf_tensor(f"xn32{i}", [128, D], F32)) for i in range(2)]
                xnt32 = [pes.enter_context(nc.sbuf_tensor(f"xnt32{i}", [128, D], F32)) for i in range(2)]
                identf1 = pes.enter_context(nc.sbuf_tensor("identf1", [128, 128], F32))
                gwb = pes.enter_context(nc.sbuf_tensor("gwb", [128, 8 * NE], F32))
                wtok = [pes.enter_context(nc.sbuf_tensor(f"wtok{i}", [128, 48], F32)) for i in range(2)]
                psT = pes.enter_context(nc.psum_tensor("psT", [128, 1024], BF16))
                psTF = [pes.enter_context(nc.psum_tensor(f"psTF{i}", [128, 1024], F32)) for i in range(2)]
                psD = [pes.enter_context(nc.psum_tensor(f"psD{i}", [128, 512], F32)) for i in range(2)]
                epsc = pes.enter_context(nc.sbuf_tensor("epsc", [128, 1], F32))
                em("v", lambda e: e.memset(epsc[:], 1e-5), SV)
                em("s", lambda e: e.dma_start(out=identf1[:], in_=id_in), DS, 16)

                em("s", lambda e: e.dma_start(
                    out=gwb[:].rearrange("p (a b) -> p a b", a=8),
                    in_=gw_in.rearrange("(a p) b -> p a b", p=128)), DS, 16)

                mks = {}

                def setm(name, sem):
                    mks[name] = (sem, mk(sem))

                def wtm(key, name):
                    if name in mks:
                        s_, v_ = mks[name]
                        if s_.dma:
                            wt(key, s_)
                        else:
                            wt(key, s_, v_)

                for it in range(NT):
                    pp = it % 2
                    wtm("s", f"lnxfree{pp}")
                    em("s", lambda e, it=it, pp=pp: e.dma_start(
                        out=lnx[pp][:], in_=x_in[it * 128:(it + 1) * 128, :]), DS, 16)
                    dsv = DS.n
                    wt("a", DS, dsv)
                    wtm("a", f"sqfree{pp}")
                    em("a", lambda e, pp=pp: e.activation(out=lnsq[pp][:], in_=lnx[pp][:], func=AF.Square), SA)
                    setm(f"sq{pp}", SA)
                    wt("v", DS, dsv)
                    wtm("v", f"lnsfree{pp}")
                    em("v", lambda e, pp=pp: e.reduce_sum(
                        out=lns[pp][:, 0:1].rearrange("p (s o) -> p s o", o=1),
                        in_=lnx[pp][:].rearrange("p (a b) -> p a b", a=1), axis=AX.X), SV)
                    wtm("v", f"sq{pp}")
                    em("v", lambda e, pp=pp: e.reduce_sum(
                        out=lns[pp][:, 1:2].rearrange("p (s o) -> p s o", o=1),
                        in_=lnsq[pp][:].rearrange("p (a b) -> p a b", a=1), axis=AX.X), SV)
                    setm(f"sqfree{pp}", SV)
                    wt("v", SV)
                    em("v", lambda e, pp=pp: e.tensor_scalar_mul(out=lns[pp][:, 2:3], in0=lns[pp][:, 0:1], scalar1=1.0 / D), SV)
                    em("v", lambda e, pp=pp: e.tensor_scalar_mul(out=lns[pp][:, 3:4], in0=lns[pp][:, 1:2], scalar1=1.0 / D), SV)
                    wt("v", SV)
                    em("v", lambda e, pp=pp: e.tensor_mul(out=lns[pp][:, 4:5], in0=lns[pp][:, 2:3], in1=lns[pp][:, 2:3]), SV)
                    wt("v", SV)
                    em("v", lambda e, pp=pp: e.tensor_sub(out=lns[pp][:, 5:6], in0=lns[pp][:, 3:4], in1=lns[pp][:, 4:5]), SV)
                    svv = mk(SV)
                    wt("a", SV, svv)
                    em("a", lambda e, pp=pp: e.activation(out=lns[pp][:, 6:7], in_=lns[pp][:, 5:6], func=AF.Sqrt, bias=epsc[:]), SA)
                    wt("v", SA)
                    em("v", lambda e, pp=pp: e.reciprocal(out=lns[pp][:, 7:8], in_=lns[pp][:, 6:7]), SV)
                    wt("v", SV)
                    wtm("v", f"x32free{pp}")
                    em("v", lambda e, pp=pp: e.tensor_scalar(
                        out=xn32[pp][:], in0=lnx[pp][:], scalar1=lns[pp][:, 2:3], scalar2=lns[pp][:, 7:8],
                        op0=OP.subtract, op1=OP.mult), SV)
                    setm(f"lnxfree{pp}", SV)
                    setm(f"lnsfree{pp}", SV)
                    if debug:
                        wt("s", SV)
                        em("s", lambda e, it=it, pp=pp: e.dma_start(
                            out=dbg["xn"][it * 128:(it + 1) * 128, :], in_=xn32[pp][:]), DS, 16)
                    x32v = mk(SV)
                    wt("t", SV, x32v)
                    for db in range(8):
                        em("t", lambda e, db=db, pp=pp: e.transpose(
                            psTF[pp][:, 128 * db:128 * (db + 1)],
                            xn32[pp][:, 128 * db:128 * (db + 1)],
                            identf1[:]), SP, wb=(psTF[pp],))
                    setm(f"x32free{pp}", SP)
                    tfv = mk(SP)
                    wt("v", SP, tfv)
                    wtm("v", f"xoutfree{pp}")
                    wtm("v", f"xnt32free{pp}")
                    em("v", lambda e, pp=pp: e.tensor_copy(out=xout[pp][:], in_=psTF[pp][:]), SV, rp=(psTF[pp],))
                    em("v", lambda e, pp=pp: e.tensor_copy(out=xnt32[pp][:], in_=psTF[pp][:]), SV, rp=(psTF[pp],))
                    xot = mk(SV)
                    wt("a", SV, xot)
                    em("a", lambda e, it=it, pp=pp: e.dma_start(
                        out=xnT_sc.rearrange("p (a b) -> p a b", a=8)[:, :, it * 128:(it + 1) * 128],
                        in_=xout[pp][:].rearrange("p (a b) -> p a b", a=8)), DA, 16)
                    setm(f"xoutfree{pp}", DA)

                    wt("t", SV, xot)
                    for db in range(8):
                        em("t", lambda e, db=db, pp=pp: e.matmul(
                            psD[pp][:, 0:8],
                            xnt32[pp][:, 128 * db:128 * (db + 1)],
                            gwb[:, 8 * db:8 * (db + 1)],
                            start=(db == 0), stop=(db == 7)), SP, wb=(psD[pp],))
                    setm(f"xnt32free{pp}", SP)
                    gv = mk(SP)
                    g = wtok[pp]
                    pd = psD[pp]
                    wt("v", SP, gv)
                    em("v", lambda e, g=g, pd=pd: e.reduce_max(
                        out=g[:, 0:1].rearrange("p (s o) -> p s o", o=1),
                        in_=pd[:, 0:8].rearrange("p (a b) -> p a b", a=1), axis=AX.X), SV, rp=(pd,))
                    wt("v", SV)
                    em("v", lambda e, g=g, pd=pd: e.tensor_scalar(
                        out=g[:, 8:16], in0=pd[:, 0:8], scalar1=g[:, 0:1], scalar2=None,
                        op0=OP.is_ge), SV, rp=(pd,))
                    wt("v", SV)
                    em("v", lambda e, g=g, pd=pd: e.scalar_tensor_tensor(
                        out=g[:, 16:24], in0=g[:, 8:16], scalar=-1e9, in1=pd[:, 0:8],
                        op0=OP.mult, op1=OP.add), SV, rp=(pd,))
                    wt("v", SV)
                    em("v", lambda e, g=g: e.reduce_max(
                        out=g[:, 1:2].rearrange("p (s o) -> p s o", o=1),
                        in_=g[:, 16:24].rearrange("p (a b) -> p a b", a=1), axis=AX.X), SV)
                    wt("v", SV)
                    em("v", lambda e, g=g: e.tensor_add(out=g[:, 2:3], in0=g[:, 0:1], in1=g[:, 1:2]), SV)
                    wt("v", SV)
                    em("v", lambda e, g=g, pd=pd: e.tensor_scalar(
                        out=g[:, 24:32], in0=pd[:, 0:8], scalar1=2.0, scalar2=g[:, 2:3],
                        op0=OP.mult, op1=OP.subtract), SV, rp=(pd,))
                    sgv = mk(SV)
                    wt("a", SV, sgv)
                    em("a", lambda e, g=g: e.activation(out=g[:, 32:40], in_=g[:, 24:32], func=AF.Sigmoid), SA)
                    wt("v", SV)
                    em("v", lambda e, g=g, pd=pd: e.tensor_scalar(
                        out=g[:, 40:48], in0=pd[:, 0:8], scalar1=g[:, 1:2], scalar2=None,
                        op0=OP.is_ge), SV, rp=(pd,))
                    wt("v", SA)
                    wt("v", SV)
                    wtm("v", f"lnxnfree{pp}")
                    em("v", lambda e, g=g, pp=pp: e.tensor_mul(out=lnxn[:, 8 * pp:8 * pp + 8],
                                                               in0=g[:, 32:40], in1=g[:, 40:48]), SV)
                    if debug:
                        wt("v", SV)
                        em("v", lambda e, g=g: e.tensor_mul(out=g[:, 0:8], in0=g[:, 32:40], in1=g[:, 40:48]), SV)
                        wt("s", SV)
                        em("s", lambda e, it=it, g=g: e.dma_start(
                            out=dbg["w"][it * 128:(it + 1) * 128, :], in_=g[:, 0:8]), DS, 16)
                    wv_ = mk(SV)
                    wt("t", SV, wv_)
                    em("t", lambda e, pp=pp: e.transpose(
                        psT[0:NE, 128 * pp:128 * (pp + 1)], lnxn[:, 8 * pp:8 * pp + 8],
                        identb[:]), SP, wb=(psT,))
                    setm(f"lnxnfree{pp}", SP)
                    trv = mk(SP)
                    wt("v", SP, trv)
                    em("v", lambda e, it=it, pp=pp: e.tensor_copy(
                        out=wTb[:, it * 128:(it + 1) * 128], in_=psT[0:NE, 128 * pp:128 * (pp + 1)]), SV, rp=(psT,))
                barrier()

            if upto < 3:
                raise _Stop
            with phase("p3") as pes:
                def sbp(name, shape, dt=BF16):
                    return pes.enter_context(nc.sbuf_tensor(name, list(shape), dt))

                xng = sbp("xng", (128, 8 * 512))
                qtok = sbp("qtok", (128, 4 * D))
                ktok = sbp("ktok", (128, 4 * D))
                vtok = sbp("vtok", (128, 4 * D))
                qT0 = sbp("qT0", (64, HH * 512))
                kT0 = sbp("kT0", (64, HH * 512))
                outT = sbp("outT", (128, 8 * 512))
                Wb32 = sbp("Wb32", (65, D), F32)
                Wba = sbp("Wba", (65, D))
                thetab = sbp("thetab", (64, HH))
                tbb = sbp("tbb", (1, HH))
                csml = sbp("csml", (128, 8), F32)
                gBc = sbp("gBc", (128, D))
                betaBc = sbp("betaBc", (128, D))
                sq_s = sbp("sq_s", (128, D), F32)
                stt = sbp("stt", (128, 128), F32)
                lrs = sbp("lrs", (128, 80), F32)
                smf = sbp("smf", (128, 80), F32)
                smb = sbp("smb", (128, 80))
                smT = sbp("smT", (80, 64))
                csb = sbp("csb", (128, 48))
                csT = sbp("csT", (48, 128))
                c0B = sbp("c0B", (128, D))
                c1B = sbp("c1B", (128, D))
                xh1 = sbp("xh1", (128, D))
                xhat = sbp("xhat", (128, D))
                tmp1 = sbp("tmp1", (128, D))
                tmp2 = sbp("tmp2", (128, D))
                e2g = sbp("e2g", (128, D))
                gp = sbp("gp", (128, 2 * D))
                g2Bc = sbp("g2Bc", (128, D))
                gxp = sbp("gxp", (128, D))
                wrm = sbp("wrm", (1, 16), F32)
                em("v", lambda e: e.memset(wrm[:], 1.0), SV)
                aB = sbp("aB", (128, D))
                bB = sbp("bB", (128, D))
                cB = sbp("cB", (128, D))
                wost = sbp("wost", (128, 8 * 128))
                woev = sbp("woev", (128, 512), F32)
                stgs = sbp("stgs", (128, 1088), F32)
                psA = pes.enter_context(nc.psum_tensor("psA", [128, 1024], F32))
                psB = pes.enter_context(nc.psum_tensor("psB", [128, 1024], F32))
                psC = pes.enter_context(nc.psum_tensor("psC", [128, 1024], F32))
                psD = pes.enter_context(nc.psum_tensor("psD3", [128, 512], F32))
                psT = pes.enter_context(nc.psum_tensor("psT3", [128, 1024], BF16))
                epsc3 = pes.enter_context(nc.sbuf_tensor("epsc3", [128, 1], F32))

                for s_sc, d_sb in ((wq_sc, wqb), (wk_sc, wkb), (wv_sc, wvb)):
                    em("s", lambda e, s_sc=s_sc, d_sb=d_sb: e.dma_start(
                        out=d_sb[:].rearrange("p (a b) -> p a b", a=8),
                        in_=s_sc.rearrange("(a p) b -> p a b", p=128)), DS, 16)
                em("v", lambda e: e.memset(epsc3[:], 1e-6), SV)
                em("s", lambda e: e.dma_start(out=Wb32[0:64, :], in_=w1_in), DS, 16)
                em("s", lambda e: e.dma_start(out=Wb32[64:65, :], in_=b1_in), DS, 16)
                em("s", lambda e: e.dma_start(out=stgs[0:64, 0:16], in_=th_in), DS, 16)
                em("s", lambda e: e.dma_start(out=stgs[0:1, 16:32], in_=tb_in), DS, 16)
                em("s", lambda e: e.dma_start(out=csml[:, 0:1], in_=inv_in), DS, 16)
                em("s", lambda e: e.dma_start(out=csml[0:64, 1:2], in_=al_in), DS, 16)
                em("s", lambda e: e.dma_start(out=csml[64:128, 1:2], in_=al_in), DS, 16)
                em("s", lambda e: e.dma_start(out=stgs[0:1, 32:32 + D], in_=ga_in), DS, 16)
                wt("v", DS)
                em("v", lambda e: e.tensor_copy(out=Wba[:], in_=Wb32[:]), SV)
                em("v", lambda e: e.tensor_copy(out=thetab[:], in_=stgs[0:64, 0:16]), SV)
                em("v", lambda e: e.tensor_copy(out=tbb[:], in_=stgs[0:1, 16:32]), SV)
                em("v", lambda e: e.tensor_add(out=csml[:, 2:3], in0=csml[:, 0:1], in1=csml[:, 1:2]), SV)
                wt("v", SV)
                em("v", lambda e: e.tensor_scalar(
                    out=csml[:, 3:4], in0=csml[:, 2:3], scalar1=0.0, scalar2=1.0 / HD,
                    op0=OP.max, op1=OP.mult), SV)
                wt("v", DS)
                em("v", lambda e: e.tensor_copy(out=xh1[0:1, :], in_=stgs[0:1, 32:32 + D]), SV)
                
                wt("t", SV)
                for cc in range(2):
                    em("t", lambda e, cc=cc: e.matmul(psA[:, 512 * cc:512 * (cc + 1)], ones1b[0:1, 0:128],
                                                      xh1[0:1, 512 * cc:512 * (cc + 1)], start=True, stop=True), SP, wb=(psA,))
                wt("v", SP)
                em("v", lambda e: e.tensor_copy(out=gBc[:], in_=psA[:]), SV, rp=(psA,))
                wt("s", SV)
                em("s", lambda e: e.dma_start(out=stgs[0:1, 32:32 + D], in_=be_in), DS, 16)
                wt("v", DS)
                em("v", lambda e: e.tensor_copy(out=tmp1[0:1, :], in_=stgs[0:1, 32:32 + D]), SV)
                wt("t", SV)
                for cc in range(2):
                    em("t", lambda e, cc=cc: e.matmul(psA[:, 512 * cc:512 * (cc + 1)], ones1b[0:1, 0:128],
                                                      tmp1[0:1, 512 * cc:512 * (cc + 1)], start=True, stop=True), SP, wb=(psA,))
                wt("v", SP)
                em("v", lambda e: e.tensor_copy(out=betaBc[:], in_=psA[:]), SV, rp=(psA,))
                wt("v", SV)
                em("v", lambda e: e.tensor_mul(out=g2Bc[:], in0=gBc[:], in1=gBc[:]), SV)
                wt("t", SV)
                em("v", lambda e: e.memset(csb[:, 16:32], 0.0), SV)
                em("v", lambda e: e.memset(smb[:, 16:32], 0.0), SV)
                em("v", lambda e: e.memset(smb[:, 48:64], 0.0), SV)

                def proj_group(gi):
                    wt("s", SP)
                    em("s", lambda e, gi=gi: e.dma_start(
                        out=xng[:].rearrange("p (a b) -> p a b", a=8),
                        in_=xnT_sc.rearrange("p (a b) -> p a b", a=8)[:, :, 512 * gi:512 * (gi + 1)]), DS, 16)
                    wt("t", DS)
                    for tb in range(4):
                        it = 4 * gi + tb
                        wt("t", SV)
                        wt("t", SA)
                        for db in range(8):
                            for ci, (pst, wsb) in enumerate(
                                    ((psA, wqb), (psB, wkb), (psC, wvb))):
                                for c in range(2):
                                    em("t", lambda e, tb=tb, db=db, pst=pst, wsb=wsb, c=c: e.matmul(
                                        pst[:, 512 * c:512 * (c + 1)],
                                        xng[:, 512 * db + 128 * tb: 512 * db + 128 * (tb + 1)],
                                        wsb[:, D * db + 512 * c: D * db + 512 * (c + 1)],
                                        start=(db == 0), stop=(db == 7)), SP, wb=(pst,))
                        wt("v", SP)
                        for pst, dst in ((psA, qtok), (psB, ktok), (psC, vtok)):
                            em("v", lambda e, pst=pst, dst=dst, tb=tb: e.tensor_copy(
                                out=dst[:, D * tb:D * (tb + 1)], in_=pst[:]), SV, rp=(pst,))
                        wt("t", SV)
                        for (src, dst) in ((qtok, qT0), (ktok, kT0)):
                            for hh in range(2):
                                for h8 in range(8):
                                    h = 8 * hh + h8
                                    em("t", lambda e, src=src, tb=tb, h=h, h8=h8: e.transpose(
                                        psT[0:64, 128 * h8:128 * (h8 + 1)],
                                        src[:, D * tb + HD * h: D * tb + HD * (h + 1)],
                                        identb[:]), SP, wb=(psT,))
                                wt("v", SP)
                                em("v", lambda e, dst=dst, tb=tb, hh=hh: e.tensor_copy(
                                    out=dst[:].rearrange("p (h t) -> p h t", h=HH)
                                        [:, 8 * hh:8 * (hh + 1), 128 * tb:128 * (tb + 1)],
                                    in_=psT[0:64, :].rearrange("p (h t) -> p h t", h=8)), SV, rp=(psT,))
                                wt("t", SV)
                    pmk["proj"] = mk(SV)

                def wo_group(gi):
                    for dm in range(8):
                        wt("a", SP)
                        em("a", lambda e, dm=dm: e.dma_start(
                            out=wost[:], in_=wo_sc.rearrange("(hb p) m -> p hb m", p=128)
                                [:, :, 128 * dm:128 * (dm + 1)]), DA, 16)
                        wt("t", DA)
                        wt("t", SV)
                        for hb in range(8):
                            em("t", lambda e, dm=dm, hb=hb: e.matmul(
                                psB[:, 0:512],
                                wost[:, 128 * hb:128 * (hb + 1)],
                                outT[:, 512 * hb:512 * (hb + 1)],
                                start=(hb == 0), stop=(hb == 7)), SP, wb=(psB,))
                        wt("v", SP)
                        wt("v", DS)
                        em("v", lambda e: e.tensor_copy(out=woev[:], in_=psB[:, 0:512]), SV, rp=(psB,))
                        wt("s", SV)
                        em("s", lambda e, dm=dm, gi=gi: e.dma_start(
                            out=st_sc[128 * dm:128 * (dm + 1), 512 * gi:512 * (gi + 1)],
                            in_=woev[:]), DS, 16)
                        wt("t", SV)

                pmk = {"proj": 0, "wall": 0, "otv": 0, "e2g": 0}
                pend = {"out": None}

                def flush_out():
                    if pend["out"] is None:
                        return
                    r, cs8, ouv = pend["out"]
                    pend["out"] = None
                    wt("t", SG, ouv)
                    for hb in range(8):
                        em("t", lambda e, r=r, hb=hb: e.transpose(
                            psT[:, 64 * hb:64 * (hb + 1)],
                            tmp1[r:r + 64, 128 * hb:128 * (hb + 1)], identb[r:r + 64, r:r + 64]), SP, wb=(psT,))
                    otv = mk(SP)
                    pmk["otv"] = otv
                    wt("v", SP, otv)
                    em("v", lambda e, cs8=cs8: e.tensor_copy(
                        out=outT[:].rearrange("p (h t) -> p h t", h=8)[:, :, 64 * cs8:64 * (cs8 + 1)],
                        in_=psT[:, 0:512].rearrange("p (h t) -> p h t", h=8)), SV, rp=(psT,))

                def ttt_step(zq, z1):
                    wt("t", SV, pmk["proj"])
                    wt("t", SV, pmk["wall"])
                    for cc in range(2):
                        em("t", lambda e, cc=cc: e.matmul(psA[:, 512 * cc:512 * (cc + 1)], ones1b[64:65, 0:128],
                                                          Wba[64:65, 512 * cc:512 * (cc + 1)], start=True, stop=True), SP, wb=(psA,))
                    em("t", lambda e: e.matmul(psD[:, 0:16], ones1b[0:1, 0:128], tbb[:], start=True, stop=True), SP, wb=(psD,))
                    for h in range(HH):
                        members = [(c, src) for c, src in ((z1, kT0), (zq, qT0)) if c is not None]
                        if "nozq" in p3cut:
                            members = [(c, s_) for c, s_ in members if s_ is kT0]
                        for mi, (c, src) in enumerate(members):
                            r = 64 * (c % 2)
                            tb = (c % 8) // 2
                            em("t", lambda e, h=h, c=c, src=src, r=r, tb=tb: e.matmul(
                                psA[r:r + 64, HD * h:HD * (h + 1)],
                                src[:, 512 * h + 128 * tb + 64 * (c % 2): 512 * h + 128 * tb + 64 * (c % 2) + 64],
                                Wba[0:64, HD * h:HD * (h + 1)],
                                start=False, stop=True, skip_group_check=True,
                                tile_position=(0, r)), SP, wb=(psA,))
                            if src is kT0 and "nolr" not in p3cut:
                                em("t", lambda e, h=h, c=c, r=r, tb=tb: e.matmul(
                                    psD[r:r + 64, h:h + 1],
                                    kT0[:, 512 * h + 128 * tb + 64 * (c % 2): 512 * h + 128 * tb + 64 * (c % 2) + 64],
                                    thetab[:, h:h + 1],
                                    start=False, stop=True, skip_group_check=True,
                                    tile_position=(0, r)), SP, wb=(psD,))
                    zv = mk(SP)
                    flush_out()
                    if "zonly" in p3cut:
                        return
                    if z1 is not None and "nograd" not in p3cut:
                        r1 = 64 * (z1 % 2)
                        tb1 = (z1 % 8) // 2
                        wt("g", SP, pmk["otv"])
                        wt("g", SV, pmk["proj"])
                        em("g", lambda e, r=r1, tb=tb1: e.tensor_sub(
                            out=tmp1[r:r + 64, :], in0=ktok[r:r + 64, D * tb:D * (tb + 1)],
                            in1=vtok[r:r + 64, D * tb:D * (tb + 1)]), SG)
                        wt("g", SG)
                        em("g", lambda e, r=r1: e.tensor_add(
                            out=tmp1[r:r + 64, :], in0=tmp1[r:r + 64, :], in1=betaBc[r:r + 64, :]), SG)
                        wt("g", SG)
                        em("g", lambda e, r=r1: e.tensor_mul(
                            out=e2g[r:r + 64, :], in0=tmp1[r:r + 64, :], in1=gBc[r:r + 64, :]), SG)
                        pmk["e2g"] = mk(SG)
                    wt("v", SP, zv)
                    em("v", lambda e: e.reduce_sum(
                        out=stt[:, 0:16].rearrange("p (s o) -> p s o", o=1),
                        in_=psA[:].rearrange("p (s k) -> p s k", s=HH), axis=AX.X), SV, rp=(psA,))
                    if z1 is not None and "nograd" not in p3cut:
                        rS = 64 * (z1 % 2)
                        wt("a", SP, zv)
                        em("a", lambda e, r=rS: e.activation(
                            out=lrs[r:r + 64, 0:16], in_=psD[r:r + 64, 0:16], func=AF.Sigmoid), SA, rp=(psD,))
                        wt("v", SA)
                        em("v", lambda e, r=rS: e.tensor_scalar_mul(
                            out=lrs[r:r + 64, 16:32], in0=lrs[r:r + 64, 0:16],
                            scalar1=csml[r:r + 64, 3:4]), SV)
                    wt("a", SP, zv)
                    wt("a", SV)
                    em("a", lambda e: e.activation(out=sq_s[:], in_=psA[:], func=AF.Square), SA, rp=(psA,))
                    em("a", lambda e: e.activation(out=wrm[:], in_=wrm[:], func=AF.Sqrt), SA)
                    wt("v", SA)
                    em("v", lambda e: e.reduce_sum(
                        out=stt[:, 16:32].rearrange("p (s o) -> p s o", o=1),
                        in_=sq_s[:].rearrange("p (s k) -> p s k", s=HH), axis=AX.X), SV)
                    wt("v", SV)
                    em("v", lambda e: e.tensor_mul(out=stt[:, 64:80], in0=stt[:, 0:16], in1=stt[:, 0:16]), SV)
                    wt("v", SV)
                    em("v", lambda e: e.scalar_tensor_tensor(
                        out=stt[:, 80:96], in0=stt[:, 16:32], scalar=float(HD), in1=stt[:, 64:80],
                        op0=OP.mult, op1=OP.subtract), SV)
                    sta = mk(SV)
                    wt("a", SV, sta)
                    em("a", lambda e: e.activation(out=stt[:, 96:112], in_=stt[:, 80:96], func=AF.Sqrt,
                                                   bias=epsc3[:], scale=1.0 / (HD * HD)), SA)
                    wt("v", SA)
                    em("v", lambda e: e.reciprocal(out=stt[:, 112:128], in_=stt[:, 96:112]), SV)
                    wt("v", SV)
                    em("v", lambda e: e.scalar_tensor_tensor(
                        out=csb[:, 0:16], in0=stt[:, 0:16], scalar=-1.0 / HD, in1=stt[:, 112:128],
                        op0=OP.mult, op1=OP.mult), SV)
                    em("v", lambda e: e.tensor_copy(out=csb[:, 32:48], in_=stt[:, 112:128]), SV)
                    wt("v", SV)
                    em("v", lambda e: e.tensor_mul(
                        out=xh1[:].rearrange("p (s k) -> p s k", s=HH),
                        in0=psA[:].rearrange("p (s k) -> p s k", s=HH),
                        in1=csb[:, 32:48].rearrange("p (s o) -> p s o", o=1).broadcast_to((128, HH, HD))), SV, rp=(psA,))
                    wt("v", SV)
                    em("v", lambda e: e.tensor_add(
                        out=xhat[:].rearrange("p (s k) -> p s k", s=HH),
                        in0=xh1[:].rearrange("p (s k) -> p s k", s=HH),
                        in1=csb[:, 0:16].rearrange("p (s o) -> p s o", o=1).broadcast_to((128, HH, HD))), SV)
                    xhv = mk(SV)
                    if z1 is not None and "nograd" not in p3cut:
                        r = 64 * (z1 % 2)
                        tb = (z1 % 8) // 2
                        wt("v", SV, xhv)
                        em("v", lambda e, r=r: e.tensor_mul(
                            out=tmp2[r:r + 64, :], in0=xhat[r:r + 64, :], in1=g2Bc[r:r + 64, :]), SV)
                        wt("v", SV)
                        wt("v", SG, pmk["e2g"])
                        em("v", lambda e, r=r: e.tensor_add(
                            out=gp[r:r + 64, 0:1024], in0=tmp2[r:r + 64, :], in1=e2g[r:r + 64, :]), SV)
                        wt("v", SV)
                        em("v", lambda e, r=r: e.tensor_mul(
                            out=gp[r:r + 64, 1024:2048], in0=gp[r:r + 64, 0:1024], in1=xhat[r:r + 64, :]), SV)
                        wt("v", SV)
                        em("v", lambda e, r=r: e.reduce_sum(
                            out=lrs[r:r + 64, 32:64].rearrange("p (s o) -> p s o", o=1),
                            in_=gp[r:r + 64, :].rearrange("p (s k) -> p s k", s=2 * HH), axis=AX.X), SV)
                        wt("v", SV)
                        em("v", lambda e, r=r: e.tensor_mul(
                            out=smb[r:r + 64, 0:16], in0=lrs[r:r + 64, 16:32], in1=stt[r:r + 64, 112:128]), SV)
                        wt("v", SV)
                        em("v", lambda e, r=r: e.scalar_tensor_tensor(
                            out=smb[r:r + 64, 32:48], in0=lrs[r:r + 64, 32:48], scalar=1.0 / HD,
                            in1=smb[r:r + 64, 0:16], op0=OP.mult, op1=OP.mult), SV)
                        em("v", lambda e, r=r: e.scalar_tensor_tensor(
                            out=smb[r:r + 64, 64:80], in0=lrs[r:r + 64, 48:64], scalar=1.0 / HD,
                            in1=smb[r:r + 64, 0:16], op0=OP.mult, op1=OP.mult), SV)
                        smv = mk(SV)
                        wt("v", SV, smv)
                        em("v", lambda e, r=r: e.tensor_mul(
                            out=tmp1[r:r + 64, :].rearrange("p (s k) -> p s k", s=HH),
                            in0=gp[r:r + 64, 0:1024].rearrange("p (s k) -> p s k", s=HH),
                            in1=smb[r:r + 64, 0:16].rearrange("p (s o) -> p s o", o=1).broadcast_to((64, HH, HD))), SV)
                        em("v", lambda e, r=r: e.tensor_mul(
                            out=tmp2[r:r + 64, :].rearrange("p (s k) -> p s k", s=HH),
                            in0=xhat[r:r + 64, :].rearrange("p (s k) -> p s k", s=HH),
                            in1=smb[r:r + 64, 64:80].rearrange("p (s o) -> p s o", o=1).broadcast_to((64, HH, HD))), SV)
                        wt("v", SV)
                        em("v", lambda e, r=r: e.tensor_sub(
                            out=tmp1[r:r + 64, :], in0=tmp1[r:r + 64, :], in1=tmp2[r:r + 64, :]), SV)
                        wt("v", SV)
                        em("v", lambda e, r=r: e.scalar_tensor_tensor(
                            out=gxp[r:r + 64, :].rearrange("p (s k) -> p s k", s=HH),
                            in0=smb[r:r + 64, 32:48].rearrange("p (s o) -> p s o", o=1).broadcast_to((64, HH, HD)),
                            scalar=-1.0,
                            in1=tmp1[r:r + 64, :].rearrange("p (s k) -> p s k", s=HH),
                            op0=OP.mult, op1=OP.add), SV)
                        gxv = mk(SV)
                        wt("t", SV, gxv)
                        for h in range(HH):
                            em("t", lambda e, h=h, r=r, tb=tb: e.matmul(
                                psC[0:64, HD * h:HD * (h + 1)],
                                ktok[r:r + 64, D * tb + HD * h: D * tb + HD * (h + 1)],
                                gxp[r:r + 64, HD * h:HD * (h + 1)],
                                start=True, stop=True), SP, wb=(psC,))
                        for h in range(HH):
                            em("t", lambda e, h=h, r=r: e.matmul(
                                psC[64:65, HD * h:HD * (h + 1)],
                                ones1b[r:r + 64, 0:1],
                                gxp[r:r + 64, HD * h:HD * (h + 1)],
                                start=True, stop=True, skip_group_check=True), SP, wb=(psC,))
                        dwv = mk(SP)
                        wt("v", SP, dwv)
                        em("v", lambda e: e.scalar_tensor_tensor(
                            out=Wba[:], in0=psC[0:65, :], scalar=-1.0, in1=Wb32[:],
                            op0=OP.mult, op1=OP.add), SV, rp=(psC,))
                        pmk["wall"] = mk(SV)
                        em("v", lambda e: e.tensor_sub(out=Wb32[:], in0=Wb32[:], in1=psC[0:65, :]), SV, rp=(psC,))
                    if zq is not None and "noout" not in p3cut:
                        r = 64 * (zq % 2)
                        tb = (zq % 8) // 2
                        cs8 = zq % 8
                        wt("g", SV, xhv)
                        em("g", lambda e, r=r: e.tensor_mul(
                            out=tmp1[r:r + 64, :], in0=xhat[r:r + 64, :], in1=gBc[r:r + 64, :]), SG)
                        wt("g", SG)
                        em("g", lambda e, r=r: e.tensor_add(
                            out=tmp2[r:r + 64, :], in0=tmp1[r:r + 64, :], in1=betaBc[r:r + 64, :]), SG)
                        wt("g", SG)
                        em("g", lambda e, r=r, tb=tb: e.tensor_add(
                            out=tmp1[r:r + 64, :], in0=tmp2[r:r + 64, :],
                            in1=qtok[r:r + 64, D * tb:D * (tb + 1)]), SG)
                        ouv = mk(SG)
                        pend["out"] = (r, cs8, ouv)

                if p3n == 0:
                    pass
                elif p3n == 1:
                    proj_group(0)
                elif p3n == 2:
                    proj_group(0)
                    ttt_step(None, 0)
                elif p3n == 3:
                    proj_group(0)
                    ttt_step(None, 0)
                    ttt_step(0, 1)
                elif p3n == 5:
                    proj_group(0)
                    ttt_step(None, 0)
                    ttt_step(None, 0)
                elif p3n == 4:
                    proj_group(0)
                    ttt_step(None, 0)
                    for c in range(0, 7):
                        ttt_step(c, c + 1)
                    ttt_step(7, None)
                    flush_out()
                    wo_group(0)
                else:
                    for gi in range(NG):
                        proj_group(gi)
                        c0 = 8 * gi
                        ttt_step(None, c0)
                        for c in range(c0, c0 + 7):
                            ttt_step(c, c + 1)
                        ttt_step(c0 + 7, None)
                        flush_out()
                        wo_group(gi)
                barrier()

        if upto < 4:
            raise _Stop
        with phase("p4") as pes:
            def sbp(name, shape, dt=BF16):
                return pes.enter_context(nc.sbuf_tensor(name, list(shape), dt))

            wacc = sbp("wacc", (128, 32 * 512))
            xnf = sbp("xnf", (128, 8 * 512))
            uco = [sbp(f"uco{e}", (64, 512)) for e in range(NE)]
            web = [sbp(f"web{e}", (128, 512)) for e in range(NE)]
            t2b = [sbp(f"t2b{i}", (128, 512)) for i in range(6)]
            mmb = [sbp(f"mmb{i}", (128, 512)) for i in range(6)]
            wab = [sbp(f"wab{i}", (128, 512)) for i in range(6)]
            wgs = [sbp(f"wgs{i}", (128, 3 * 1024)) for i in range(3)]
            wus = [sbp(f"wus{i}", (128, 3 * 1024)) for i in range(3)]
            bt1 = [sbp(f"bt1{i}", (32, 3 * 1024)) for i in range(3)]
            bt3 = [sbp(f"bt3{i}", (64, 3 * 1024)) for i in range(3)]
            a2s = [sbp(f"a2s{i}", (128, 3 * 128)) for i in range(3)]
            wds = [sbp(f"wds{i}", (128, 32 * 128)) for i in range(2)]
            zsb = [sbp(f"zsb{g}", (128, 512)) for g in range(2)]
            dnev = [sbp(f"dnev{i}", (128, 512), F32) for i in range(2)]
            zrow = sbp("zrow", (1, 128))
            em("v", lambda e: e.memset(zrow[:], 0.0), SV)
            for gz in range(2):
                em("v", lambda e, gz=gz: e.memset(zsb[gz][:], 0.0), SV)
            psD1 = [pes.enter_context(nc.psum_tensor(f"psD1{i}", [128, 512], F32)) for i in range(3)]
            psD3 = [pes.enter_context(nc.psum_tensor(f"psD3{i}", [128, 512], F32)) for i in range(3)]
            psZ = [pes.enter_context(nc.psum_tensor(f"psZ{g}", [128, 512], F32)) for g in range(2)]
            psM = psD1
            DWq = [DW, DW1, DW2]
            DDq = [DD0, DD1]
            DGq = [DGa, DGb]

            NJ = 32 if p4n < 0 else p4n
            NTC = T // 512

            def qload(gt, gate):
                sl = gt % 3
                kk = gt % 11
                if gate is not None:
                    wt("s", SP, gate)
                for j in [j_ for j_ in (3 * kk, 3 * kk + 1, 3 * kk + 2) if j_ < 32]:
                    jj = j % 3
                    co = 1024 * jj
                    em("s", lambda e, j=j, sl=sl, co=co: e.dma_start(
                        out=wgs[sl][:, co:co + 1024].rearrange("p (a b) -> p a b", a=8),
                        in_=wg_sc.rearrange("(a p) f -> p a f", p=128)[:, :, 128 * j:128 * (j + 1)]), DWq[sl], 16)
                    em("s", lambda e, j=j, sl=sl, co=co: e.dma_start(
                        out=wus[sl][:, co:co + 1024].rearrange("p (a b) -> p a b", a=8),
                        in_=wu_sc.rearrange("(a p) f -> p a f", p=128)[:, :, 128 * j:128 * (j + 1)]), DWq[sl], 16)
                    em("s", lambda e, j=j, sl=sl, co=co: e.dma_start(
                        out=bt1[sl][0:16, co:co + 1024].rearrange("p (a b) -> p a b", a=8),
                        in_=b1pos_sc.rearrange("p (a f) -> p a f", a=8)[:, :, 128 * j:128 * (j + 1)]), DWq[sl], 16)
                    em("s", lambda e, j=j, sl=sl, co=co: e.dma_start(
                        out=bt1[sl][16:32, co + 128:co + 1024].rearrange("p (a b) -> p a b", a=7),
                        in_=b1pos_sc.rearrange("p (a f) -> p a f", a=8)[:, 0:7, 128 * j:128 * (j + 1)]), DWq[sl], 16)
                    em("s", lambda e, j=j, sl=sl, co=co: e.dma_start(
                        out=bt3[sl][32:48, co:co + 1024].rearrange("p (a b) -> p a b", a=8),
                        in_=b3pos_sc.rearrange("p (a f) -> p a f", a=8)[:, :, 128 * j:128 * (j + 1)]), DWq[sl], 16)
                    em("s", lambda e, j=j, sl=sl, co=co: e.dma_start(
                        out=bt3[sl][48:64, co + 128:co + 1024].rearrange("p (a b) -> p a b", a=7),
                        in_=b3pos_sc.rearrange("p (a f) -> p a f", a=8)[:, 0:7, 128 * j:128 * (j + 1)]), DWq[sl], 16)
                    em("s", lambda e, j=j, sl=sl, jj=jj: e.dma_start(
                        out=a2s[sl][:, 128 * jj:128 * (jj + 1)],
                        in_=a2_sc[128 * j:128 * (j + 1), :]), DWq[sl], 16)

            def up_mm(j, tc):
                sl, jj = (11 * tc + j // 3) % 3, j % 3
                pj = j % 3
                wt("t", DWq[sl])
                for db in range(8):
                    em("t", lambda e, pj=pj, sl=sl, jj=jj, db=db: e.matmul(
                        psD1[pj][:, :], wgs[sl][:, 1024 * jj + 128 * db:1024 * jj + 128 * (db + 1)],
                        xnf[:, 512 * db:512 * (db + 1)],
                        start=(db == 0), stop=(db == 7)), SP, wb=(psD1[pj],))
                for db in range(8):
                    em("t", lambda e, pj=pj, sl=sl, jj=jj, db=db: e.matmul(
                        psD3[pj][:, :], wus[sl][:, 1024 * jj + 128 * db:1024 * jj + 128 * (db + 1)],
                        xnf[:, 512 * db:512 * (db + 1)],
                        start=(db == 0), stop=(db == 7)), SP, wb=(psD3[pj],))

            def chain_triple(js, tc):
                tm1 = {}
                tm3 = {}
                am = {}

                def zgroup(e_):
                    for j in js:
                        sl, jj = (11 * tc + j // 3) % 3, j % 3
                        bi = 2 * (j % 3) + (e_ % 2)
                        wt("t", SV)
                        em("t", lambda e, e_=e_, bi=bi, sl=sl, jj=jj: e.matmul(
                            psZ[e_ // 4][32 * (e_ % 4):32 * (e_ % 4) + 16, :],
                            a2s[sl][:, 128 * jj + 16 * e_:128 * jj + 16 * (e_ + 1)], wab[bi][:],
                            start=False, stop=True,
                            skip_group_check=True,
                            tile_position=(0, 32 * (e_ % 4))), SP, wb=(psZ[e_ // 4],))
                        zlast[bi] = mk(SP)

                for e_ in range(NE):
                    for j in js:
                        sl, jj = (11 * tc + j // 3) % 3, j % 3
                        pj = j % 3
                        if e_ == 0:
                            em("t", lambda e, pj=pj, sl=sl, jj=jj: e.matmul(
                                psD1[pj][:, :], bt1[sl][0:16, 1024 * jj:1024 * jj + 128], uco[0][0:16, :],
                                start=False, stop=True, skip_group_check=True), SP, wb=(psD1[pj],))
                        else:
                            em("t", lambda e, pj=pj, sl=sl, jj=jj, e_=e_: e.matmul(
                                psD1[pj][:, :], bt1[sl][0:32, 1024 * jj + 128 * e_:1024 * jj + 128 * (e_ + 1)],
                                uco[e_][0:32, :], start=False, stop=True, skip_group_check=True), SP, wb=(psD1[pj],))
                        tm1[j] = mk(SP)
                    for j in js:
                        sl, jj = (11 * tc + j // 3) % 3, j % 3
                        pj = j % 3
                        if e_ == 0:
                            em("t", lambda e, pj=pj, sl=sl, jj=jj: e.matmul(
                                psD3[pj][:, :], bt3[sl][32:48, 1024 * jj:1024 * jj + 128], uco[0][32:48, :],
                                start=False, stop=True, skip_group_check=True,
                                tile_position=(32, 0)), SP, wb=(psD3[pj],))
                        else:
                            em("t", lambda e, pj=pj, sl=sl, jj=jj, e_=e_: e.matmul(
                                psD3[pj][:, :], bt3[sl][32:64, 1024 * jj + 128 * e_:1024 * jj + 128 * (e_ + 1)],
                                uco[e_][32:64, :], start=False, stop=True, skip_group_check=True,
                                tile_position=(32, 0)), SP, wb=(psD3[pj],))
                        tm3[j] = mk(SP)
                    if e_ > 0:
                        zgroup(e_ - 1)
                    for j in js:
                        pj = j % 3
                        bi = 2 * (j % 3) + (e_ % 2)
                        wt("a", SP, tm1[j])
                        pv = vlast.get(bi)
                        if pv is not None:
                            wt("a", SV, pv)
                        em("a", lambda e, pj=pj, bi=bi: e.activation(
                            out=t2b[bi][:], in_=psD1[pj][:], func=AF.Silu), SA, rp=(psD1[pj],))
                        am[j] = mk(SA)
                    for j in js:
                        pj = j % 3
                        bi = 2 * (j % 3) + (e_ % 2)
                        wt("v", SA, am[j])
                        wt("v", SP, tm3[j])
                        em("v", lambda e, pj=pj, bi=bi: e.tensor_mul(
                            out=mmb[bi][:], in0=t2b[bi][:], in1=psD3[pj][:]), SV, rp=(psD3[pj],))
                        vlast[bi] = mk(SV)
                    for j in js:
                        bi = 2 * (j % 3) + (e_ % 2)
                        wt("v", SV)
                        zl = zlast.get(bi)
                        if zl is not None:
                            wt("v", SP, zl)
                        gl = glast.get(bi)
                        if gl is not None:
                            wt("v", SG, gl[1])
                        if e_ == 0:
                            em("v", lambda e, j=j, bi=bi: e.tensor_mul(
                                out=wacc[:, 512 * j:512 * (j + 1)], in0=mmb[bi][:], in1=web[0][:]), SV)
                            wt("v", SV)
                            em("v", lambda e, j=j, bi=bi: e.tensor_copy(
                                out=wab[bi][:], in_=wacc[:, 512 * j:512 * (j + 1)]), SV)
                        else:
                            em("v", lambda e, e_=e_, bi=bi: e.tensor_mul(
                                out=wab[bi][:], in0=mmb[bi][:], in1=web[e_][:]), SV)
                            wt("g", SV)
                            pj1 = gj.get(j)
                            if pj1 is not None:
                                wt("g", SG, pj1[1])
                            em("g", lambda e, j=j, bi=bi: e.tensor_add(
                                out=wacc[:, 512 * j:512 * (j + 1)],
                                in0=wacc[:, 512 * j:512 * (j + 1)], in1=wab[bi][:]), SG)
                            gj[j] = glast[bi] = ("g", mk(SG))
                zgroup(NE - 1)

            qload(0, None)
            qload(1, None)
            qload(2, None)
            dnm = {}
            dgn = {}
            trimark = {}
            zlast = {}
            glast = {}
            vlast = {}
            gj = {}

            for tc in range(NTC):
                if tc == 0:
                    em("s", lambda e, tc=tc: e.dma_start(
                        out=xnf[:].rearrange("p (a b) -> p a b", a=8),
                        in_=xnT_sc.rearrange("p (a b) -> p a b", a=8)[:, :, 512 * tc:512 * (tc + 1)]), DS, 16)
                wt("t", DS)
                wt("t", SV)
                for gz in range(2):
                    em("t", lambda e, gz=gz: e.matmul(
                        psZ[gz][:, :], zrow[:], wTb[0:1, 0:512], start=True, stop=True), SP, wb=(psZ[gz],))
                for e_ in range(NE):
                    wt("t", SV)
                    for db in range(8):
                        em("t", lambda e, e_=e_, db=db, tc=tc: e.matmul(
                            psM[e_ % 2][0:64, :],
                            acb[:, 512 * db + 64 * e_: 512 * db + 64 * (e_ + 1)],
                            xnf[:, 512 * db:512 * (db + 1)],
                            start=(db == 0), stop=(db == 7)), SP, wb=(psM[e_ % 2],))
                    wt("v", SP)
                    em("v", lambda e, e_=e_: e.tensor_copy(out=uco[e_][:], in_=psM[e_ % 2][0:64, :]), SV, rp=(psM[e_ % 2],))
                for e_ in range(NE):
                    wt("t", SV)
                    em("t", lambda e, e_=e_, tc=tc: e.matmul(
                        psM[e_ % 2][:, :], pat8b[:, 128 * e_:128 * (e_ + 1)],
                        wTb[:, 512 * tc:512 * (tc + 1)], start=True, stop=True), SP, wb=(psM[e_ % 2],))
                    wt("v", SP)
                    em("v", lambda e, e_=e_: e.tensor_copy(out=web[e_][:], in_=psM[e_ % 2][:]), SV, rp=(psM[e_ % 2],))

                gj.clear()
                for j in range(min(3, NJ)):
                    up_mm(j, tc)
                ktc = (NJ + 2) // 3
                for k in range(ktc):
                    js = [j for j in (3 * k, 3 * k + 1, 3 * k + 2) if j < NJ]
                    chain_triple(js, tc)
                    trimark[(tc, k)] = mk(SP)
                    gt = 11 * tc + k
                    if p4n < 0 and gt + 3 < 11 * NTC:
                        qload(gt + 3, trimark[(tc, k)])
                    for j in (3 * k + 3, 3 * k + 4, 3 * k + 5):
                        if j < NJ:
                            up_mm(j, tc)
                if p4n >= 0:
                    barrier()
                    raise _Stop
                if tc + 1 < NTC:
                    wt("s", SP, mk(SP))
                    em("s", lambda e, tc=tc: e.dma_start(
                        out=xnf[:].rearrange("p (a b) -> p a b", a=8),
                        in_=xnT_sc.rearrange("p (a b) -> p a b", a=8)[:, :, 512 * (tc + 1):512 * (tc + 2)]), DS, 16)
                wt("t", SG)
                wt("t", DAC)
                wt("v", SP)
                for gz in range(2):
                    for jj in range(4):
                        em("v", lambda e, gz=gz, jj=jj: e.tensor_copy(
                            out=zsb[gz][32 * jj:32 * jj + 16, :],
                            in_=psZ[gz][32 * jj:32 * jj + 16, :]), SV, rp=(psZ[gz],))

                def wdload(dm, gate):
                    sl = dm % 2
                    if gate is not None:
                        wt("s", SP, gate)
                    em("s", lambda e, dm=dm, sl=sl: e.dma_start(
                        out=wds[sl][:].rearrange("p (a b) -> p a b", a=32),
                        in_=wd_sc.rearrange("(a p) m -> p a m", p=128)[:, :, 128 * dm:128 * (dm + 1)]), DDq[sl], 16)

                wdload(0, dnm.get((tc - 1, 6)))
                wdload(1, dnm.get((tc - 1, 7)))
                for dm in range(8):
                    pd = dm % 2
                    wt("t", DDq[pd])
                    wt("t", SV)
                    for j in range(32):
                        em("t", lambda e, j=j, pd=pd: e.matmul(
                            psM[pd][:, :], wds[pd][:, 128 * j:128 * (j + 1)],
                            wacc[:, 512 * j:512 * (j + 1)], start=(j == 0), stop=(j == 31)), SP, wb=(psM[pd],))
                    for gz in range(2):
                        em("t", lambda e, gz=gz, dm=dm, pd=pd: e.matmul(
                            psM[pd][:, :], b2s[gz][:, 128 * dm:128 * (dm + 1)], zsb[gz][:],
                            start=False, stop=True, skip_group_check=True), SP, wb=(psM[pd],))
                    dnm[(tc, dm)] = mk(SP)
                    if dm + 2 < 8:
                        wdload(dm + 2, dnm[(tc, dm)])
                    wt("v", SP)
                    wt("v", DGq[pd])
                    em("v", lambda e, pd=pd: e.tensor_copy(out=dnev[pd][:], in_=psM[pd][:]), SV, rp=(psM[pd],))
                    wt("g", SV)
                    em("g", lambda e, dm=dm, tc=tc, pd=pd: e.dma_start(
                        out=st_sc[128 * dm:128 * (dm + 1), 512 * tc:512 * (tc + 1)],
                        in_=dnev[pd][:], accum_op=OP.add), DGq[pd], 16)
            barrier()

        if upto < 5:
            raise _Stop
        with phase("p5") as pes:
            fx = pes.enter_context(nc.sbuf_tensor("fx", [128, D], F32))
            fst = pes.enter_context(nc.sbuf_tensor("fst", [128, D], F32))
            fo = pes.enter_context(nc.sbuf_tensor("fo", [128, D], F32))
            identf = pes.enter_context(nc.sbuf_tensor("identf", [128, 128], F32))
            psF = pes.enter_context(nc.psum_tensor("psF", [128, 1024], F32))

            em("s", lambda e: e.dma_start(out=identf[:], in_=id_in), DS, 16)
            if debug:
                em("s", lambda e: e.dma_start(out=dbg["st"], in_=st_sc), DS, 16)
            for it in range(NT):
                wt("s", SV)
                em("s", lambda e, it=it: e.dma_start(
                    out=fx[:], in_=x_in[it * 128:(it + 1) * 128, :]), DS, 16)
                em("s", lambda e, it=it: e.dma_start(
                    out=fst[:].rearrange("p (a b) -> p a b", a=8),
                    in_=st_sc.rearrange("(a p) t -> p a t", p=128)[:, :, 128 * it:128 * (it + 1)]), DS, 16)
                wt("t", DS)
                wt("t", SV)
                for db in range(8):
                    em("t", lambda e, db=db: e.transpose(
                        psF[:, 128 * db:128 * (db + 1)], fst[:, 128 * db:128 * (db + 1)],
                        identf[:]), SP, wb=(psF,))
                wt("v", SP)
                wt("v", DS)
                em("v", lambda e: e.tensor_add(out=fo[:], in0=fx[:], in1=psF[:]), SV, rp=(psF,))
                wt("s", SV)
                em("s", lambda e, it=it: e.dma_start(
                    out=y_out[it * 128:(it + 1) * 128, :], in_=fo[:]), DS, 16)
            for k in EK:
                wtall(k)
    except _Stop:
        pass

    return nc


def make_inputs(full, core):
    b = core
    d = {}
    d["x"] = np.ascontiguousarray(full["L0_attn_hidden_states"][b])
    d["wq"] = np.asarray(full["L0_attn_wq"])
    d["wk"] = np.asarray(full["L0_attn_wk"])
    d["wv"] = np.asarray(full["L0_attn_wv"])
    d["wo"] = np.asarray(full["L0_attn_wo"])
    W1 = np.asarray(full["L0_attn_W1"])
    d["w1s"] = np.ascontiguousarray(W1.transpose(1, 0, 2).reshape(HD, HH * HD))
    d["b1s"] = np.ascontiguousarray(np.asarray(full["L0_attn_b1"]).reshape(1, HH * HD))
    d["thetas"] = np.ascontiguousarray(np.asarray(full["L0_attn_theta"])[:, :, 0].T)
    d["tbs"] = np.ascontiguousarray(np.asarray(full["L0_attn_theta_bias"]).reshape(1, HH))
    d["alphas"] = np.ascontiguousarray(np.asarray(full["L0_attn_alpha"]).reshape(CH, 1))
    d["gammas"] = np.ascontiguousarray(np.asarray(full["L0_attn_gamma"]).reshape(1, HH * HD))
    d["betas"] = np.ascontiguousarray(np.asarray(full["L0_attn_beta"]).reshape(1, HH * HD))
    d["gwt"] = np.ascontiguousarray(np.asarray(full["L0_ffn_gate_w"]).T)
    d["wgt"] = np.ascontiguousarray(np.asarray(full["L0_ffn_w_gate"]).T)
    d["wut"] = np.ascontiguousarray(np.asarray(full["L0_ffn_w_up"]).T)
    d["wdt"] = np.ascontiguousarray(np.asarray(full["L0_ffn_w_down"]).T)
    A1 = np.asarray(full["L0_ffn_A1"]); A3 = np.asarray(full["L0_ffn_A3"])
    ac = np.zeros((D, NE * 64), np.float32)
    for e in range(NE):
        ac[:, 64 * e + 0:64 * e + 16] = A1[e].T
        if e > 0:
            ac[:, 64 * e + 16:64 * e + 32] = A1[e - 1].T
        ac[:, 64 * e + 32:64 * e + 48] = A3[e].T
        if e > 0:
            ac[:, 64 * e + 48:64 * e + 64] = A3[e - 1].T
    d["acomb"] = ac
    B1 = np.asarray(full["L0_ffn_B1"]); B3 = np.asarray(full["L0_ffn_B3"])
    d["b1p"] = np.ascontiguousarray(B1.transpose(2, 0, 1).reshape(NR, NE * FF))
    d["b3p"] = np.ascontiguousarray(B3.transpose(2, 0, 1).reshape(NR, NE * FF))
    A2 = np.asarray(full["L0_ffn_A2"])
    d["a2sc"] = np.ascontiguousarray(A2.transpose(2, 0, 1).reshape(FF, NE * NR))
    B2 = np.asarray(full["L0_ffn_B2"])
    b2tp = np.zeros((2 * 128, D), np.float32)
    for e in range(NE):
        g = e // 4
        jj = e % 4
        b2tp[128 * g + 32 * jj:128 * g + 32 * jj + NR, :] = B2[e].T
    d["b2tp"] = b2tp
    d["identc"] = np.eye(128, dtype=np.float32)
    d["onesc"] = np.ones((128, 128), np.float32)
    d["invkc"] = (1.0 / (np.arange(128) % CH + 1.0)).astype(np.float32).reshape(128, 1)
    p8 = np.zeros((NE, NE * 128), np.float32)
    for e in range(NE):
        p8[e, 128 * e:128 * (e + 1)] = 1.0
    d["pat8c"] = p8
    p16 = np.zeros((HH, HH * HD), np.float32)
    for h in range(HH):
        p16[h, HD * h:HD * (h + 1)] = 1.0
    d["pat16c"] = np.tile(p16, (5, 1))
    return d


def slice_T(d, T):
    d = dict(d)
    d["x"] = d["x"][:T]
    return d


def kernel(**inputs):
    nc = build(T=4096)
    in_maps = [make_inputs(inputs, b) for b in range(NCORES)]
    res = run_bass_kernel_spmd(nc, in_maps, core_ids=list(range(NCORES)))
    out = np.stack([np.asarray(res.results[b]["y"]) for b in range(NCORES)], axis=0)
    return out.astype(np.float32)
